# version 31
# baseline (speedup 1.0000x reference)
import contextlib
import math

import numpy as np
import ml_dtypes

import concourse.bass as bass
import concourse.mybir as mybir
from concourse.bass_utils import run_bass_kernel_spmd

F32 = mybir.dt.float32
BF16 = mybir.dt.bfloat16
AF = mybir.ActivationFunctionType
ALU = mybir.AluOpType
AX = mybir.AxisListType

E = 2048
H = 16
NH = 8
D = 128
S = 2048
NC = 16
NT = 16
QH = 1024
INV_SQRT_D = 1.0 / math.sqrt(D)
OFFS = 8.0


def _slopes():
    return [2.0 ** (-0.5 * (i + 1)) for i in range(H)]


def _build_crow():
    sl = _slopes()
    qq = np.arange(S).astype(np.float64)
    rows = [(-math.sqrt(D) * sl[h] * qq).astype(np.float32) for h in range(H)]
    flat = np.concatenate(rows).astype(ml_dtypes.bfloat16)
    return flat.reshape(1, -1)


def _host_consts():
    sl = _slopes()
    alibi = np.zeros((128, H * 16), np.float32)
    p = np.arange(128, dtype=np.float64)
    for h in range(H):
        for kt in range(16):
            alibi[:, h * 16 + kt] = (sl[h] * (kt * 128 + p) + OFFS).astype(np.float32)
    lt = np.where(np.arange(128)[:, None] <= np.arange(128)[None, :],
                  np.float32(0.0), np.float32(-1e9)).astype(np.float32)
    rep = np.zeros((8, 128), np.float32)
    for e_ in range(8):
        rep[e_, e_ * 16:(e_ + 1) * 16] = 1.0
    rep = rep.astype(ml_dtypes.bfloat16)
    return alibi, lt, rep, np.eye(8, dtype=np.float32), np.eye(128, dtype=np.float32)


class Prog:
    ENGS = ("sync", "tensor", "scalar", "vector", "gpsimd")
    SEM = {"tensor": ("pe", 1), "scalar": ("act", 1), "vector": ("dve", 1)}
    DSEMS = ("sdc", "sd0", "sd1", "sc0", "sc1", "ga0", "ga1", "wo0", "wo1",
             "sa0", "sa1", "sx0", "sx1", "go0", "go1", "ccA", "ccB1", "ccB2")

    def __init__(self, nc, ctx):
        self.nc = nc
        self.thunks = {e: [] for e in self.ENGS}
        self.sems = {}
        self.cnt = {}
        for name in ("pe", "act", "dve") + self.DSEMS:
            self.sems[name] = ctx.enter_context(nc.semaphore(name))
            self.cnt[name] = 0

    def emit(self, eng, make, waits=(), inc=False, dsem=None):
        wl = []
        for t in waits:
            if t is None:
                continue
            s, v = t
            wl.append((self.sems[s], v))
        tok = None
        incs = None
        if inc:
            if eng in ("sync", "gpsimd"):
                assert dsem is not None
                sname, by = dsem, (1 if dsem.startswith("cc") else 16)
            else:
                sname, by = self.SEM[eng]
            self.cnt[sname] += by
            tok = (sname, self.cnt[sname])
            incs = (self.sems[sname], by)

        def thunk(e, make=make, wl=wl, incs=incs):
            for (s, v) in wl:
                e.wait_ge(s, v)
            ins = make(e)
            if incs is not None:
                ins.then_inc(incs[0], incs[1])

        self.thunks[eng].append(thunk)
        return tok

    def waits(self, eng, waits):
        wl = [(self.sems[s], v) for t in waits if t is not None for (s, v) in [t]]
        if not wl:
            return

        def thunk(e, wl=wl):
            for (s, v) in wl:
                e.wait_ge(s, v)

        self.thunks[eng].append(thunk)

    def flush_block(self):
        thunks = self.thunks
        self.thunks = {e: [] for e in self.ENGS}
        with self.nc.Block() as blk:
            for eng in self.ENGS:
                if thunks[eng]:
                    def runner(e, lst=thunks[eng]):
                        for t in lst:
                            t(e)
                    getattr(blk, eng)(runner)


def tmax(*toks):
    best = {}
    for t in toks:
        if t is None:
            continue
        s, v = t
        if s not in best or best[s] < v:
            best[s] = v
    return [(s, v) for s, v in best.items()]


def build_nc():
    nc = bass.Bass("TRN2", target_bir_lowering=False)

    d_xfm = nc.dram_tensor("xfm", [128, NC * S], F32, kind="ExternalInput")
    d_xres = nc.dram_tensor("xres", [QH, E], F32, kind="ExternalInput")
    d_wq = nc.dram_tensor("wq", [128, NH * NC * D], F32, kind="ExternalInput")
    d_wk = nc.dram_tensor("wk", [128, NH * NC * D], F32, kind="ExternalInput")
    d_wv = nc.dram_tensor("wv", [128, 2 * NC * 512], F32, kind="ExternalInput")
    d_wo = nc.dram_tensor("wo", [128, NC * E], F32, kind="ExternalInput")
    d_bqc = nc.dram_tensor("bqc", [128, NH], F32, kind="ExternalInput")
    d_bkc = nc.dram_tensor("bkc", [128, NH], F32, kind="ExternalInput")
    d_bvr = nc.dram_tensor("bvr", [1, NH * D], F32, kind="ExternalInput")
    d_bor = nc.dram_tensor("bor", [1, E], F32, kind="ExternalInput")
    d_wr = nc.dram_tensor("wr", [128, NC * 33], F32, kind="ExternalInput")
    d_br = nc.dram_tensor("br", [8, 1], F32, kind="ExternalInput")
    d_wd = nc.dram_tensor("wd", [128, NC * 128], F32, kind="ExternalInput")
    d_bd = nc.dram_tensor("bd", [128, 1], F32, kind="ExternalInput")
    d_wu = nc.dram_tensor("wu", [128, E], F32, kind="ExternalInput")
    d_bu = nc.dram_tensor("bu", [8, E], F32, kind="ExternalInput")
    d_alibi = nc.dram_tensor("alibi", [128, NH * 16], F32, kind="ExternalInput")
    d_lt = nc.dram_tensor("lt", [128, 128], F32, kind="ExternalInput")
    d_rep = nc.dram_tensor("repm", [8, 128], BF16, kind="ExternalInput")
    d_id8 = nc.dram_tensor("id8", [8, 8], F32, kind="ExternalInput")
    d_id128 = nc.dram_tensor("id128", [128, 128], F32, kind="ExternalInput")
    d_crow = nc.dram_tensor("crow", [1, NH * S], BF16, kind="ExternalInput")
    d_sela = nc.dram_tensor("sela", [128, 1], F32, kind="ExternalInput")
    d_selb = nc.dram_tensor("selb", [128, 1], F32, kind="ExternalInput")
    d_out = nc.dram_tensor("out", [QH, E], F32, kind="ExternalOutput")
    d_aoscr = [nc.dram_tensor("aoscr0", [NH * D, QH], BF16),
               nc.dram_tensor("aoscr1a", [4 * D, QH], BF16),
               nc.dram_tensor("aoscr1b", [4 * D, QH], BF16)]
    d_hscr = nc.dram_tensor("hscr", [128, S], BF16)
    d_wtscr = nc.dram_tensor("wtscr", [8, S], BF16)
    d_ccout = [nc.dram_tensor("ccout0", [E, QH], BF16),
               nc.dram_tensor("ccout1a", [NH * D, QH], BF16),
               nc.dram_tensor("ccout1b", [NH * D, QH], BF16)]

    with contextlib.ExitStack() as ctx:
        en = ctx.enter_context
        P = Prog(nc, ctx)
        emit = P.emit

        stage = [en(nc.sbuf_tensor(f"stage{i}", [128, 2048], F32)) for i in range(2)]
        ones1b = en(nc.sbuf_tensor("ones1b", [1, 128], BF16))
        ones1f = en(nc.sbuf_tensor("ones1f", [1, 128], F32))
        onescb = en(nc.sbuf_tensor("onescb", [128, 128], BF16))
        epsc = en(nc.sbuf_tensor("epsc", [1, 1], F32))
        alibi = en(nc.sbuf_tensor("alibi_s", [128, NH * 16], F32))
        lowtri = en(nc.sbuf_tensor("lowtri", [128, 128], F32))
        bqc = en(nc.sbuf_tensor("bqc_s", [128, NH], F32))
        bkc = en(nc.sbuf_tensor("bkc_s", [128, NH], F32))
        bvrb = en(nc.sbuf_tensor("bvrb", [1, NH * D], BF16))
        sela = en(nc.sbuf_tensor("sela_s", [128, 1], F32))
        selb = en(nc.sbuf_tensor("selb_s", [128, 1], F32))

        psA = en(nc.psum_tensor("psA", [128, QH], F32))
        psB = en(nc.psum_tensor("psB", [128, QH], F32))
        psPV = en(nc.psum_tensor("psPV", [128, QH], F32))
        psD1 = en(nc.psum_tensor("psD1", [128, 512], F32))
        psD2 = en(nc.psum_tensor("psD2", [128, 512], F32))
        SC = [psA, psB]
        REGS = [("A0", psA, 0), ("B0", psB, 0), ("A1", psA, 512), ("B1", psB, 512)]
        bank = {}

        def mm(out_ap, lhsT_ap, rhs_ap, start, stop):
            return lambda e: e.matmul(out_ap, lhsT_ap, rhs_ap, start=start, stop=stop)

        def dma(dst, src):
            return lambda e: e.dma_start(dst, src)

        ctxm = contextlib.ExitStack()
        em = ctxm.enter_context
        wrf = em(nc.sbuf_tensor("wrf", [128, NC * 33], F32))
        onescf = em(nc.sbuf_tensor("onescf", [128, 1], F32))
        wsumr = em(nc.sbuf_tensor("wsumr", [8, 1], F32))
        brp = em(nc.sbuf_tensor("brp", [8, 1], F32))
        brpb = em(nc.sbuf_tensor("brpb", [8, 1], F32))
        Vg = [en(nc.sbuf_tensor("Vg0", [128, 8192], BF16, side="right"))]
        ctx_x = contextlib.ExitStack()
        ex = ctx_x.enter_context
        xbf = [ex(nc.sbuf_tensor(f"xbf{c}", [128, S], BF16, side="right"))
               for c in range(NC)]
        wkq_b = [ex(nc.sbuf_tensor(f"wkq_b{i}", [128, 2048], BF16, side="right"))
                 for i in range(2)]
        wv_b = ex(nc.sbuf_tensor("wv_b", [128, 8192], BF16, side="right"))
        ctx_lg = contextlib.ExitStack()
        lgraw = ctx_lg.enter_context(nc.sbuf_tensor("lgraw", [8, S], F32))
        lgfin = ctx_lg.enter_context(nc.sbuf_tensor("lgfin", [8, S], F32))

        for dst, src in [
            (alibi[:, :], d_alibi[:, :]), (lowtri[:, :], d_lt[:, :]),
            (bqc[:, :], d_bqc[:, :]), (bkc[:, :], d_bkc[:, :]),
            (stage[0][0:1, 0:NH * D], d_bvr[:, :]), (wrf[:, :], d_wr[:, :]),
            (sela[:, :], d_sela[:, :]), (selb[:, :], d_selb[:, :]),
        ]:
            emit("sync", dma(dst, src), inc=True, dsem="sdc")
        tok_c0 = emit("sync", dma(brp[:, :], d_br[:, :]), inc=True, dsem="sdc")
        for tens, val in [(ones1b, 1.0), (ones1f, 1.0), (onescb, 1.0),
                          (onescf, 1.0), (epsc, 1e-5)]:
            emit("vector", lambda e, t=tens, v=val: e.memset(t[:, :], v), inc=False)
        emit("vector", lambda e: e.tensor_copy(bvrb[:, :], stage[0][0:1, 0:NH * D]),
             waits=[tok_c0], inc=False)
        emit("vector", lambda e: e.tensor_scalar(brpb[:, :], brp[:, :], 2.9994, None, ALU.add),
             waits=[tok_c0], inc=True)
        P.waits("vector", [tok_c0])
        P.flush_block()

        LREG = [psA[0:33, 0:512], psA[0:33, 512:QH],
                psB[0:33, 0:512], psB[0:33, 512:QH]]
        LG8 = [psA[0:8, 0:512], psA[0:8, 512:QH],
               psB[0:8, 0:512], psB[0:8, 512:QH]]
        STAT_XSQ = [psPV[0:8, 0:512], psPV[0:8, 512:QH],
                    psD1[0:8, :], psD2[0:8, :]]
        ctx_sq = contextlib.ExitStack()
        xsqb = [ctx_sq.enter_context(nc.sbuf_tensor(f"xsq{i}", [128, 512], BF16))
                for i in range(4)]
        sq_guard = [None] * 4
        last_cast = [None, None]
        last_lgmm = [None, None]
        for c in range(NC):
            j = c % 2
            td = emit("sync", dma(stage[j][:, :], d_xfm[:, c * S:(c + 1) * S]),
                      waits=tmax(last_cast[j], last_lgmm[j]), inc=True, dsem=f"sd{j}")
            last_cast[j] = emit(
                "vector", lambda e, c=c, j=j: e.tensor_copy(xbf[c][:, :], stage[j][:, :]),
                waits=[td], inc=True)
            for tcn in range(4):
                last_lgmm[j] = emit("tensor", mm(
                    LREG[tcn], wrf[:, c * 33:(c + 1) * 33],
                    stage[j][:, tcn * 512:(tcn + 1) * 512],
                    c == 0, c == NC - 1),
                    waits=(tmax(td, tok_c0) if tcn == 0 else [td]),
                    inc=(tcn == 3))
            for tcn in range(4):
                tsq = emit("vector", lambda e, c=c, tcn=tcn: e.tensor_mul(
                    xsqb[tcn][:, :], xbf[c][:, tcn * 512:(tcn + 1) * 512],
                    xbf[c][:, tcn * 512:(tcn + 1) * 512]),
                    waits=tmax(last_cast[j], sq_guard[tcn]), inc=True)
                sq_guard[tcn] = emit("tensor", mm(
                    STAT_XSQ[tcn], onescb[:, 0:8], xsqb[tcn][:, :],
                    c == 0, c == NC - 1), waits=[tsq], inc=True)
        stats_done = sq_guard[3]
        for tcn in range(4):
            emit("vector", lambda e, tcn=tcn: e.tensor_copy(
                lgraw[0:8, tcn * 512:(tcn + 1) * 512], LG8[tcn]),
                waits=[last_lgmm[1]] if tcn == 0 else [], inc=False)
        P.flush_block()

        ctx_sq.close()
        ctx1 = contextlib.ExitStack()
        if True:
            e1 = ctx1.enter_context
            a_rep = e1(nc.sbuf_tensor("a_rep", [128, S], F32))
            ma_rep = e1(nc.sbuf_tensor("ma_rep", [128, S], F32))
            arow = e1(nc.sbuf_tensor("arow", [1, S], F32))
            marow = e1(nc.sbuf_tensor("marow", [1, S], F32))
            trow = e1(nc.sbuf_tensor("trow", [1, 512], F32))
            t2row = e1(nc.sbuf_tensor("t2row", [1, 512], F32))
            t3row = e1(nc.sbuf_tensor("t3row", [1, 512], F32))
            SROW = [(psA[32:33, 0:512], psPV[0:1, 0:512]),
                    (psA[32:33, 512:QH], psPV[0:1, 512:QH]),
                    (psB[32:33, 0:512], psD1[0:1, :]),
                    (psB[32:33, 512:QH], psD2[0:1, :])]
            t_wsum = None
            for c in range(NC):
                t_wsum = emit("tensor", mm(
                    psD2[32:40, 0:1], wrf[:, c * 33:c * 33 + 8],
                    onescf[:, 0:1], c == 0, c == NC - 1),
                    inc=(c == NC - 1))
            t_wsum_c = emit("vector", lambda e: e.tensor_copy(
                wsumr[:, :], psD2[32:40, 0:1]), waits=[t_wsum], inc=True)
            rep_toks = []
            prev_r5 = None
            prev_r6 = None
            prev_r7 = None
            for tcn in range(4):
                cs = slice(tcn * 512, (tcn + 1) * 512)
                xs_r, xq_r = SROW[tcn]
                r1 = emit("vector", lambda e, xs_r=xs_r: e.tensor_scalar(
                    trow[:, :], xs_r, 1.0 / E, None, ALU.mult),
                    waits=(tmax(stats_done) if tcn == 0 else tmax(prev_r7)),
                    inc=True)
                r2 = emit("vector", lambda e, xq_r=xq_r: e.tensor_scalar(
                    t2row[:, :], xq_r, 1.0 / E, None, ALU.mult),
                    waits=tmax(r1, prev_r5), inc=True)
                r3 = emit("vector", lambda e: e.tensor_mul(t3row[:, :], trow[:, :], trow[:, :]),
                          waits=tmax(r2, prev_r6), inc=True)
                r4 = emit("vector", lambda e: e.tensor_sub(t2row[:, :], t2row[:, :], t3row[:, :]),
                          waits=[r3], inc=True)
                r5 = emit("scalar", lambda e: e.activation(
                    t3row[:, :], t2row[:, :], AF.Ln, bias=epsc[0:1, 0:1], scale=1.0),
                    waits=[r4], inc=True)
                r6 = emit("scalar", lambda e, cs=cs: e.activation(
                    arow[0:1, cs], t3row[:, :], AF.Exp, bias=0.0, scale=-0.5),
                    waits=[r5], inc=True)
                r7 = emit("vector", lambda e, cs=cs: e.tensor_mul(
                    marow[0:1, cs], trow[:, :], arow[0:1, cs]),
                    waits=[r6], inc=True)
                prev_r5, prev_r6, prev_r7 = r5, r6, r7
                rep_toks.append((r6, r7))
            REP_A = [psA[:, 0:512], psA[:, 512:QH], psB[:, 0:512], psB[:, 512:QH]]
            REP_B = [psPV[:, 0:512], psPV[:, 512:QH], psD1[:, :], psD2[:, :]]
            tca_l = []
            last_rep = None
            for tcn in range(4):
                cs = slice(tcn * 512, (tcn + 1) * 512)
                ta = emit("tensor", mm(REP_A[tcn], ones1f[0:1, :], arow[0:1, cs],
                                       True, True),
                          waits=tmax(rep_toks[tcn][0]), inc=True)
                tca = emit("vector", lambda e, cs=cs, tcn=tcn: e.tensor_copy(
                    a_rep[:, cs], REP_A[tcn]), waits=[ta], inc=True)
                tca_l.append(tca)
                tb = emit("tensor", mm(REP_B[tcn], ones1f[0:1, :], marow[0:1, cs],
                                       True, True),
                          waits=tmax(rep_toks[tcn][1],
                                     (t_wsum_c if tcn == 3 else None)), inc=True)
                tcb = emit("vector", lambda e, cs=cs, tcn=tcn: e.tensor_copy(
                    ma_rep[:, cs], REP_B[tcn]), waits=[tb], inc=True)
                last_rep = tcb
            lg2 = e1(nc.sbuf_tensor("lg2", [8, S], F32))
            f1 = emit("vector", lambda e: e.tensor_mul(lgfin[:, :], lgraw[:, :], a_rep[0:8, :]),
                      waits=[last_rep], inc=True)
            f2 = emit("vector", lambda e: e.scalar_tensor_tensor(
                lg2[:, :], ma_rep[0:8, :], wsumr[0:8, 0:1], lgfin[:, :],
                ALU.mult, ALU.subtract), waits=tmax(f1, t_wsum_c), inc=True)
            f3 = emit("vector", lambda e: e.tensor_scalar(
                lgfin[:, :], lg2[:, :], -1.0, brpb[0:8, 0:1], ALU.mult, ALU.add),
                waits=[f2], inc=True)

        ctx2 = contextlib.ExitStack()
        e2 = ctx2.enter_context
        repm = e2(nc.sbuf_tensor("repm_s", [8, 128], BF16))
        id8 = e2(nc.sbuf_tensor("id8_s", [8, 8], F32))
        id128 = e2(nc.sbuf_tensor("id128_s", [128, 128], F32))
        bdc = e2(nc.sbuf_tensor("bdc", [128, 1], F32))
        wd_b = e2(nc.sbuf_tensor("wd_b", [128, NC * 128], BF16))
        hpr = e2(nc.sbuf_tensor("hpr", [128, S], BF16))
        wTb = e2(nc.sbuf_tensor("wTb", [8, S], BF16))
        gtok = e2(nc.sbuf_tensor("gtok", [128, 128], F32))
        wtok = e2(nc.sbuf_tensor("wtok", [128, 128], F32))
        g_m1 = e2(nc.sbuf_tensor("g_m1", [128, 16], F32))
        g_m2 = e2(nc.sbuf_tensor("g_m2", [128, 16], F32))
        g_k1 = e2(nc.sbuf_tensor("g_k1", [128, 128], F32))
        g_k2 = e2(nc.sbuf_tensor("g_k2", [128, 128], F32))
        g_sd = e2(nc.sbuf_tensor("g_sd", [128, 16], F32))
        g_sd1 = e2(nc.sbuf_tensor("g_sd1", [128, 16], F32))
        g_gg = e2(nc.sbuf_tensor("g_gg", [128, 128], F32))
        for dst, src in [
            (repm[:, :], d_rep[:, :]), (id8[:, :], d_id8[:, :]),
            (id128[:, :], d_id128[:, :]), (bdc[:, :], d_bd[:, :]),
        ]:
            ldt = emit("sync", dma(dst, src), inc=True, dsem="sdc")
        t_wd = emit("sync", dma(stage[0][:, :], d_wd[:, :]), inc=True, dsem="sd0")
        t_wdc = emit("vector", lambda e: e.tensor_copy(wd_b[:, :], stage[0][:, :]),
                     waits=[t_wd], inc=True)
        stage_use = [t_wdc, None]
        vload_tok = {}

        def v_load(g, wv_guard):
            tcv = None
            for j4 in range(4):
                jj = j4 % 2
                tdv = emit("sync", dma(stage[jj][:, :],
                                       d_wv[:, (g * 4 + j4) * 2048:(g * 4 + j4 + 1) * 2048]),
                           waits=[stage_use[jj]], inc=True, dsem=f"sd{jj}")
                tcv = emit("scalar", lambda e, j4=j4, jj=jj: e.copy(
                    wv_b[:, j4 * 2048:(j4 + 1) * 2048], stage[jj][:, :]),
                    waits=tmax(tdv, (wv_guard if j4 == 0 else None)), inc=True)
                stage_use[jj] = tcv
            vload_tok[g] = tcv

        v_load(0, None)
        bank["A0"] = tca_l[0]
        bank["A1"] = tca_l[1]
        bank["B0"] = tca_l[2]
        bank["B1"] = tca_l[3]
        rr = 0

        def reg_next():
            nonlocal rr
            name, t, o = REGS[rr % 4]
            rr += 1
            return name, t, o

        t_gt = None
        for t in range(NT):
            name, pt, o = reg_next()
            reg = pt[:, o:o + 8]
            ttr = emit("tensor", lambda e, t=t, reg=reg: e.transpose(
                reg, lgfin[0:8, t * 128:(t + 1) * 128], id8[:, :]),
                waits=tmax(bank[name], ldt, f3), inc=True)
            t_gt = emit("vector", lambda e, t=t, reg=reg: e.tensor_copy(
                gtok[:, t * 8:(t + 1) * 8], reg), waits=[ttr], inc=True)
            bank[name] = t_gt

        def b3(tens):
            return bass.AP(tens, 0, [[128, 128], [8, 16], [1, 8]])

        def b3out(tens):
            return bass.AP(tens, 0, [[16, 128], [1, 16], [1, 1]])

        def bc3(tens):
            return bass.AP(tens, 0, [[16, 128], [1, 16], [0, 8]])

        q1 = emit("vector", lambda e: e.tensor_reduce(b3out(g_m1), b3(gtok), AX.X, ALU.max),
                  waits=[t_gt], inc=True)
        q2 = emit("vector", lambda e: e.tensor_tensor(b3(g_k1), b3(gtok), bc3(g_m1), ALU.is_ge),
                  waits=[q1], inc=True)
        q3 = emit("vector", lambda e: e.scalar_tensor_tensor(
            b3(g_gg), b3(g_k1), -1e9, b3(gtok), ALU.mult, ALU.add), waits=[q2], inc=True)
        q4 = emit("vector", lambda e: e.tensor_reduce(b3out(g_m2), b3(g_gg), AX.X, ALU.max),
                  waits=[q3], inc=True)
        q5 = emit("vector", lambda e: e.tensor_tensor(b3(g_k2), b3(g_gg), bc3(g_m2), ALU.is_ge),
                  waits=[q4], inc=True)
        q6 = emit("vector", lambda e: e.tensor_sub(g_sd1[:, :], g_m2[:, :], g_m1[:, :]),
                  waits=[q5], inc=True)
        q7 = emit("scalar", lambda e: e.activation(g_sd[:, :], g_sd1[:, :], AF.Sigmoid,
                                                   bias=0.0, scale=1.0), waits=[q6], inc=True)
        q8 = emit("vector", lambda e: e.tensor_scalar(g_sd1[:, :], g_sd[:, :], -1.0, 1.0,
                                                      ALU.mult, ALU.add), waits=[q7], inc=True)
        q9 = emit("vector", lambda e: e.tensor_tensor(wtok[:, :], g_k1[:, :], bc3(g_sd1), ALU.mult),
                  waits=[q8], inc=True)
        qa = emit("vector", lambda e: e.tensor_tensor(g_gg[:, :], g_k2[:, :], bc3(g_sd), ALU.mult),
                  waits=[q9], inc=True)
        qb = emit("vector", lambda e: e.tensor_add(wtok[:, :], wtok[:, :], g_gg[:, :]),
                  waits=[qa], inc=True)
        t_wt = None
        for t in range(NT):
            name, pt, o = reg_next()
            reg = pt[0:8, o:o + 128]
            ttr = emit("tensor", lambda e, t=t, reg=reg: e.transpose(
                reg, wtok[:, t * 8:(t + 1) * 8], id128[:, :]),
                waits=tmax(bank[name], qb), inc=True)
            t_wt = emit("vector", lambda e, t=t, reg=reg: e.tensor_copy(
                wTb[0:8, t * 128:(t + 1) * 128], reg), waits=[ttr], inc=True)
            bank[name] = t_wt
        t_wtb = t_wt
        xn_tok = {}
        for c in range(NC):
            x1 = emit("vector", lambda e, c=c: e.tensor_mul(
                xbf[c][:, :], xbf[c][:, :], a_rep[:, :]),
                waits=[last_rep] if c == 0 else [], inc=True)
            xn_tok[c] = emit("vector", lambda e, c=c: e.tensor_sub(
                xbf[c][:, :], xbf[c][:, :], ma_rep[:, :]), waits=[x1], inc=True)
        HREGS = [psPV[:, 0:512], psPV[:, 512:QH], psD1[:, :], psD2[:, :]]
        rtail = last_rep
        tw_toks = []
        for tcn in range(4):
            nW, ptW, oW = reg_next()
            regW = ptW[:, oW:oW + 512]
            tw = emit("tensor", mm(regW, repm[0:8, :],
                                   bass.AP(wTb, tcn * 512, [[S, 8], [1, 512]]), True, True),
                      waits=tmax(bank[nW], t_wtb, ldt), inc=True)
            tw_toks.append((nW, regW, tw))
        htok = [None] * 4
        for c in range(NC):
            for tcn in range(4):
                w = []
                if tcn == 0:
                    w = tmax(xn_tok[c], t_wdc, (rtail if c == 0 else None))
                htok[tcn] = emit("tensor", mm(
                    HREGS[tcn], wd_b[:, c * 128:(c + 1) * 128],
                    bass.AP(xbf[c], tcn * 512, [[S, 128], [1, 512]]),
                    c == 0, c == NC - 1), waits=w, inc=(c == NC - 1))
        for tcn in range(4):
            cs = slice(tcn * 512, (tcn + 1) * 512)
            nW, regW, tw = tw_toks[tcn]
            j = tcn % 2
            trl = emit("scalar", lambda e, j=j, tcn=tcn: e.activation(
                stage[j][:, 0:512], HREGS[tcn], AF.Relu, bias=bdc[:, 0:1], scale=1.0),
                waits=tmax(htok[tcn], stage_use[j]), inc=True)
            thp = emit("vector", lambda e, cs=cs, j=j, regW=regW: e.tensor_tensor(
                hpr[:, cs], regW, stage[j][:, 0:512], ALU.mult),
                waits=[trl, tw], inc=True)
            stage_use[j] = thp
            bank[nW] = thp
        for t in range(NT):
            name, pt, o = reg_next()
            reg = pt[:, o:o + 512]
            tok = None
            for c in range(NC):
                w = tmax(bank[name], vload_tok[0]) if c == 0 else []
                tok = emit("tensor", mm(reg, xbf[c][:, t * 128:(t + 1) * 128],
                                        wv_b[:, c * 512:(c + 1) * 512],
                                        c == 0, False), waits=w, inc=False)
            tok = emit("tensor", mm(reg, ones1b[0:1, :],
                                    bvrb[0:1, 0:512], False, True), inc=True)
            bank[name] = emit("vector", lambda e, t=t, reg=reg: e.tensor_copy(
                Vg[0][:, t * 512:(t + 1) * 512], reg), waits=[tok], inc=True)
        th1 = emit("sync", dma(d_hscr[:, :], hpr[:, :]), waits=[thp], inc=True,
                   dsem="ga0")
        th2 = emit("sync", dma(d_wtscr[:, :], wTb[:, :]), waits=[t_wtb], inc=True,
                   dsem="ga1")
        P.waits("sync", [th1, th2])
        P.flush_block()
        ctx2.close()
        ctx1.close()
        ctx_lg.close()

        Kh = [em(nc.sbuf_tensor(f"Kh{i}", [128, S], BF16)) for i in range(NH)]
        Qh = [em(nc.sbuf_tensor(f"Qh{i}", [128, S], BF16)) for i in range(NH)]
        Vg.append(em(nc.sbuf_tensor("Vg1", [128, 8192], BF16)))
        for name, *_ in REGS:
            bank[name] = None
        stage_use = [None, None]
        wkq_use = [None, None]
        rr = 0
        tcv = None
        for j4 in range(4):
            jj = j4 % 2
            tdv = emit("sync", dma(stage[jj][:, :],
                                   d_wv[:, (4 + j4) * 2048:(5 + j4) * 2048]),
                       waits=[stage_use[jj]], inc=True, dsem=f"sd{jj}")
            tcv = emit("scalar", lambda e, j4=j4, jj=jj: e.copy(
                wv_b[:, j4 * 2048:(j4 + 1) * 2048], stage[jj][:, :]),
                waits=[tdv], inc=True)
            stage_use[jj] = tcv
        for t in range(NT):
            name, pt, o = REGS[rr % 4]
            rr += 1
            reg = pt[:, o:o + 512]
            tok = None
            for c in range(NC):
                w = tmax(bank[name], tcv) if c == 0 else []
                tok = emit("tensor", mm(reg, xbf[c][:, t * 128:(t + 1) * 128],
                                        wv_b[:, c * 512:(c + 1) * 512],
                                        c == 0, False), waits=w, inc=False)
            tok = emit("tensor", mm(reg, ones1b[0:1, :],
                                    bvrb[0:1, 512:1024], False, True), inc=True)
            bank[name] = emit("vector", lambda e, t=t, reg=reg: e.tensor_copy(
                Vg[1][:, t * 512:(t + 1) * 512], reg), waits=[tok], inc=True)
        for g in range(2):
            for hl in range(4):
                h = g * 4 + hl
                for wi, (which, dsrc, bias_col, dest) in enumerate((
                        ("k", d_wk, bkc, Kh), ("q", d_wq, bqc, Qh))):
                    jj = (hl * 2 + wi) % 2
                    tdw = emit("sync", dma(stage[jj][:, :],
                                           dsrc[:, h * 2048:(h + 1) * 2048]),
                               waits=[stage_use[jj]], inc=True, dsem=f"sd{jj}")
                    tcw = emit("scalar", lambda e, jj=jj: e.copy(
                        wkq_b[jj][:, :], stage[jj][:, :]),
                        waits=tmax(tdw, wkq_use[jj]), inc=True)
                    stage_use[jj] = tcw
                    for tcn in range(4):
                        name, pt, o = REGS[rr % 4]
                        rr += 1
                        reg = pt[:, o:o + 512]
                        tok = None
                        for c in range(NC):
                            w = tmax(bank[name], tcw) if c == 0 else []
                            tok = emit("tensor", mm(
                                reg, wkq_b[jj][:, c * 128:(c + 1) * 128],
                                bass.AP(xbf[c], tcn * 512, [[S, 128], [1, 512]]),
                                c == 0, c == NC - 1), waits=w, inc=(c == NC - 1))
                        wkq_use[jj] = tok
                        bank[name] = emit("vector", lambda e, tcn=tcn, reg=reg, h=h,
                                          bias_col=bias_col, dest=dest: e.tensor_scalar(
                            dest[h][:, tcn * 512:(tcn + 1) * 512], reg,
                            bias_col[:, h:h + 1], None, ALU.add),
                            waits=[tok], inc=True)
        P.flush_block()
        ctx_x.close()
        wo_b = en(nc.sbuf_tensor("wo_b", [128, NC * E], BF16, side="right"))
        expT = [em(nc.sbuf_tensor(f"expT{i}", [128, QH], BF16)) for i in range(2)]
        rep_sb = em(nc.sbuf_tensor("rep_sb", [128, QH], F32))
        ao_sb = [em(nc.sbuf_tensor(f"ao{i}", [128, QH], BF16)) for i in range(2)]
        crow_sb = [em(nc.sbuf_tensor(f"crow{i}", [1, S], BF16)) for i in range(2)]
        rep_use = (None, None)

        for name, *_ in REGS:
            bank[name] = None
        bank["SCf"] = [None, None]
        bank["PV"] = None
        bank["D1"] = None
        bank["D2"] = None
        ai = 0
        ao_store = {}
        crow_use = [None, None]
        ci = 0
        tok_cc = {}
        wo_dma = [None] * NC
        wo_tok = [None] * NC

        def wo_load(c):
            j = c % 2
            wo_dma[c] = emit("gpsimd", dma(stage[j][:, :],
                                           d_wo[:, c * E:(c + 1) * E]),
                             waits=[wo_tok[c - 2]] if c >= 2 else [],
                             inc=True, dsem=f"wo{j}")

        def wo_cast(c):
            j = c % 2
            wo_tok[c] = emit("vector", lambda e, c=c, j=j: e.tensor_copy(
                wo_b[:, c * E:(c + 1) * E], stage[j][:, :]),
                waits=[wo_dma[c]], inc=True)

        wo_load(0)
        wo_load(1)
        sched = [(0, hh) for hh in range(NH)] + \
                [(1, hh) for hh in (4, 5, 6, 7, 0, 1, 2, 3)]
        tdc_toks = {0: emit("sync", dma(crow_sb[0][0:1, 0:S],
                                        d_crow[0:1, sched[0][1] * S:(sched[0][1] + 1) * S]),
                            inc=True, dsem="sc0")}
        for ii, (qh, h) in enumerate(sched):
            Kt = 8 if qh == 0 else 16
            q0 = qh * QH
            if True:
                g, hl = h // 4, h % 4
                cb = ii % 2
                tdc = tdc_toks[ii]
                exp_toks = {}
                eff_toks = {}
                pv_toks = {}

                def pieces(qs):
                    if qs < 512:
                        return [(qs, 512 - qs), (512, 512)]
                    return [(qs, QH - qs)]

                def pieces_acc(kt, Kt=Kt, q0=q0):
                    qs = max(kt * 128 - q0, 0)
                    lk0 = (q0 + 512) // 128 - 1
                    lk1 = Kt - 1
                    segs = []
                    if kt * 128 >= q0:
                        segs.append([qs, 128])
                        rest = qs + 128
                    else:
                        rest = qs
                    while rest < QH:
                        end = 512 if rest < 512 else QH
                        segs.append([rest, end - rest])
                        rest = end
                    out = []
                    seen = set()
                    for off, nn in segs:
                        bk = 0 if off < 512 else 1
                        start = (kt == 0) and bk not in seen
                        seen.add(bk)
                        stop = (kt == (lk0 if bk == 0 else lk1))
                        out.append((off, nn, start, stop))
                    return out

                def emit_pv(kt, Kt=Kt, q0=q0, g=g, hl=hl, eff_toks=eff_toks,
                            pv_toks=pv_toks, pieces_acc=pieces_acc):
                    w0 = tmax(eff_toks[kt],
                              *( [bank["PV"], bank["D1"], bank["D2"]]
                                 if kt == 0 else []))
                    tok = None
                    pcs = pieces_acc(kt)
                    for idx, (off, nn, st, stp) in enumerate(pcs):
                        tok = emit("tensor", mm(
                            psPV[:, off:off + nn],
                            Vg[g][:, kt * 512 + hl * 128:kt * 512 + hl * 128 + 128],
                            expT[kt % 2][:, off:off + nn],
                            st, stp),
                            waits=(w0 if idx == 0 else []), inc=False)
                    for idx, (off, nn, st, stp) in enumerate(pcs):
                        if off < 512:
                            dreg = psD1[:, off:off + nn]
                        else:
                            dreg = psD2[:, off - 512:off - 512 + nn]
                        tok = emit("tensor", mm(
                            dreg, onescb[:, :], expT[kt % 2][:, off:off + nn],
                            st, stp), inc=(idx == len(pcs) - 1))
                    pv_toks[kt] = tok
                    return tok

                for kt in range(Kt):
                    qs = max(kt * 128 - q0, 0)
                    sc = SC[kt % 2]
                    scn = "A" if kt % 2 == 0 else "B"
                    w0 = tmax(exp_toks.get(kt - 2),
                              bank["SCf"][kt % 2],
                              *( [bank[scn + "0"], bank[scn + "1"], tdc]
                                 if kt < 2 else []))
                    for idx, (off, nn) in enumerate(pieces(qs)):
                        emit("tensor", mm(
                            sc[:, off:off + nn],
                            Kh[h][:, kt * 128:(kt + 1) * 128],
                            Qh[h][:, q0 + off:q0 + off + nn],
                            True, False), waits=(w0 if idx == 0 else []), inc=False)
                    tok_sc = None
                    pcs = pieces(qs)
                    for idx, (off, nn) in enumerate(pcs):
                        tok_sc = emit("tensor", mm(
                            sc[:, off:off + nn], ones1b[0:1, :],
                            crow_sb[cb][0:1, q0 + off:q0 + off + nn],
                            False, True), inc=(idx == len(pcs) - 1))
                    if kt >= 1:
                        emit_pv(kt - 1)
                    t_mk = None
                    if kt * 128 >= q0:
                        t_mk = emit("vector", lambda e, kt=kt, qs=qs, sc=sc: e.tensor_tensor(
                            sc[:, qs:qs + 128], sc[:, qs:qs + 128],
                            lowtri[:, :], ALU.add),
                            waits=[tok_sc], inc=True)
                    t_exp = emit("scalar", lambda e, kt=kt, qs=qs, sc=sc, h=h: e.activation(
                        expT[kt % 2][:, qs:QH], sc[:, qs:QH], AF.Exp,
                        bias=alibi[:, h * 16 + kt:h * 16 + kt + 1],
                        scale=INV_SQRT_D),
                        waits=tmax(tok_sc, t_mk, pv_toks.get(kt - 2)), inc=True)
                    exp_toks[kt] = t_exp
                    eff_toks[kt] = t_exp
                emit_pv(Kt - 1)
                tln1 = emit("scalar", lambda e: e.activation(
                    rep_sb[:, 0:512], psD1[:, :], AF.Ln, bias=0.0, scale=1.0),
                    waits=tmax(pv_toks[Kt - 1], rep_use[0]), inc=True)
                tln2 = emit("scalar", lambda e: e.activation(
                    rep_sb[:, 512:QH], psD2[:, :], AF.Ln, bias=0.0, scale=1.0),
                    waits=tmax(rep_use[1]), inc=True)
                trc1 = emit("scalar", lambda e: e.activation(
                    rep_sb[:, 0:512], rep_sb[:, 0:512], AF.Exp,
                    bias=0.0, scale=-1.0), waits=[tln1], inc=True)
                trc2 = emit("scalar", lambda e: e.activation(
                    rep_sb[:, 512:QH], rep_sb[:, 512:QH], AF.Exp,
                    bias=0.0, scale=-1.0), waits=[tln2], inc=True)
                bank["D1"] = tln1
                bank["D2"] = tln2
                t_ao1 = emit("vector", lambda e, ai=ai: e.tensor_tensor(
                    ao_sb[ai % 2][:, 0:512], psPV[:, 0:512], rep_sb[:, 0:512],
                    ALU.mult), waits=tmax(trc1, ao_store.get(ai - 2)), inc=True)
                t_ao = emit("vector", lambda e, ai=ai: e.tensor_tensor(
                    ao_sb[ai % 2][:, 512:QH], psPV[:, 512:QH], rep_sb[:, 512:QH],
                    ALU.mult), waits=[trc2], inc=True)
                rep_use = (t_ao1, t_ao)
                if qh == 0:
                    ao_dst = d_aoscr[0][h * 128:(h + 1) * 128, :]
                elif h < 4:
                    ao_dst = d_aoscr[1][h * 128:(h + 1) * 128, :]
                else:
                    ao_dst = d_aoscr[2][(h - 4) * 128:(h - 3) * 128, :]
                if ii + 1 < len(sched):
                    nqh, nh = sched[ii + 1]
                    tdc_toks[ii + 1] = emit(
                        "sync", dma(crow_sb[(ii + 1) % 2][0:1, 0:S],
                                    d_crow[0:1, nh * S:(nh + 1) * S]),
                        waits=[crow_use[(ii + 1) % 2]], inc=True,
                        dsem=f"sc{(ii + 1) % 2}")
                t_st = emit("sync", dma(ao_dst, ao_sb[ai % 2][:, :]),
                            waits=[t_ao], inc=True, dsem=f"ga{ai % 2}")
                ao_store[ai] = t_st
                ai += 1
                crow_use[cb] = tok_sc
                bank["SCf"][Kt % 2] = exp_toks[Kt - 2]
                bank["SCf"][(Kt - 1) % 2] = exp_toks[Kt - 1]
                bank["PV"] = t_ao
                for n4 in ("A0", "A1", "B0", "B1"):
                    bank[n4] = None
                wo_cast(ii)
                if ii + 2 < NC:
                    wo_load(ii + 2)
                if ii == 7:
                    tok_cc["A"] = emit("gpsimd", lambda e: e.collective_compute(
                        "AllGather", ALU.bypass,
                        replica_groups=[[0, 1], [2, 3], [4, 5], [6, 7]],
                        ins=[d_aoscr[0].ap().opt()], outs=[d_ccout[0].ap().opt()]),
                        waits=tmax(ao_store[ai - 1], ao_store[ai - 2]),
                        inc=True, dsem="ccA")
                if ii == 11:
                    tok_cc["B2"] = emit("gpsimd", lambda e: e.collective_compute(
                        "AllGather", ALU.bypass,
                        replica_groups=[[0, 1], [2, 3], [4, 5], [6, 7]],
                        ins=[d_aoscr[2].ap().opt()], outs=[d_ccout[2].ap().opt()]),
                        waits=tmax(ao_store[ai - 1], ao_store[ai - 2]),
                        inc=True, dsem="ccB2")
        tok_cc["B1"] = emit("gpsimd", lambda e: e.collective_compute(
            "AllGather", ALU.bypass,
            replica_groups=[[0, 1], [2, 3], [4, 5], [6, 7]],
            ins=[d_aoscr[1].ap().opt()], outs=[d_ccout[1].ap().opt()]),
            waits=tmax(ao_store[ai - 1], ao_store[ai - 2]),
            inc=True, dsem="ccB1")
        P.flush_block()

        ctxm.close()

        borb = en(nc.sbuf_tensor("borb", [1, E], BF16))
        wu_b = en(nc.sbuf_tensor("wu_b", [128, E], BF16))
        bu_b = en(nc.sbuf_tensor("bu_b", [8, E], BF16))
        hpr_my = en(nc.sbuf_tensor("hpr_my", [128, QH], BF16))
        wTb_my = en(nc.sbuf_tensor("wTb_my", [8, QH], BF16))
        hprf = en(nc.sbuf_tensor("hprf", [128, S], BF16))
        wtbf = en(nc.sbuf_tensor("wtbf", [8, S], BF16))
        aotl = [en(nc.sbuf_tensor(f"aotl{i}", [128, 2048], BF16)) for i in range(2)]
        aoth = [en(nc.sbuf_tensor(f"aoth{i}", [128, 2048], BF16)) for i in range(2)]
        aot = [en(nc.sbuf_tensor(f"aot{i}", [128, 2048], BF16)) for i in range(2)]
        xrs = [en(nc.sbuf_tensor(f"xrs{i}", [128, 2048], F32)) for i in range(2)]
        outb = [en(nc.sbuf_tensor(f"outb{i}", [128, 2048], F32)) for i in range(2)]
        gsil = [en(nc.sbuf_tensor(f"gsil{i}", [128, 512], F32)) for i in range(2)]
        t_hl = emit("sync", dma(hprf[:, :], d_hscr[:, :]), inc=True, dsem="sdc")
        t_wl = emit("sync", dma(wtbf[:, :], d_wtscr[:, :]), inc=True, dsem="sdc")
        t_hsel0 = emit("vector", lambda e: e.tensor_scalar(
            hpr_my[:, :], hprf[:, 0:QH], sela[:, 0:1], None, ALU.mult),
            waits=[t_wl], inc=True)
        t_hsel = emit("vector", lambda e: e.scalar_tensor_tensor(
            hpr_my[:, :], hprf[:, QH:S], selb[:, 0:1], hpr_my[:, :],
            ALU.mult, ALU.add), waits=[t_hsel0], inc=True)
        t_wsel0 = emit("vector", lambda e: e.tensor_scalar(
            wTb_my[:, :], wtbf[0:8, 0:QH], sela[0:8, 0:1], None, ALU.mult),
            waits=[t_hsel], inc=True)
        t_wsel = emit("vector", lambda e: e.scalar_tensor_tensor(
            wTb_my[:, :], wtbf[0:8, QH:S], selb[0:8, 0:1], wTb_my[:, :],
            ALU.mult, ALU.add), waits=[t_wsel0], inc=True)
        t_bo = emit("sync", dma(stage[0][0:1, 0:E], d_bor[:, :]), inc=True, dsem="sd0")
        t_bu = emit("sync", dma(stage[0][64:72, 0:E], d_bu[:, :]), inc=True, dsem="sd0")
        t_wu = emit("sync", dma(stage[1][:, 0:E], d_wu[:, :]), inc=True, dsem="sd1")
        t_bob = emit("vector", lambda e: e.tensor_copy(borb[:, :], stage[0][0:1, 0:E]),
                     waits=[t_bu], inc=True)
        t_bub = emit("vector", lambda e: e.tensor_copy(bu_b[:, :], stage[0][64:72, 0:E]),
                     waits=[t_bu], inc=True)
        t_wub = emit("vector", lambda e: e.tensor_copy(wu_b[:, :], stage[1][:, 0:E]),
                     waits=[t_wu], inc=True)
        for name, *_ in REGS:
            bank[name] = None
        aot_use = [None, None]
        sel_use = [None, None]
        xrs_use = [None, None]
        outb_use = [None, None]
        gsil_use = [None, None]
        aot_dma = {}
        out_toks = []
        rr = 0
        t_out = None

        def aot_load(t):
            j = t % 2
            tk = None
            for hh in range(2):
                src = bass.AP(d_ccout[0], (hh * 8 * 128) * QH + t * 128,
                              [[QH, 128], [128 * QH, 8], [1, 128]])
                dst = bass.AP(aotl[j], hh * 1024, [[2048, 128], [128, 8], [1, 128]])
                tk = emit("sync", dma(dst, src),
                          waits=(tmax(sel_use[j], tok_cc["A"]) if hh == 0 else []),
                          inc=True, dsem=f"sa{j}")
            for part, rowbase, c0, ccn in ((2, 0, 4, "B2"), (2, 512, 12, "B2"),
                                           (1, 0, 0, "B1"), (1, 512, 8, "B1")):
                src = bass.AP(d_ccout[part], rowbase * QH + t * 128,
                              [[QH, 128], [128 * QH, 4], [1, 128]])
                dst = bass.AP(aoth[j], c0 * 128, [[2048, 128], [128, 4], [1, 128]])
                tk = emit("sync", dma(dst, src),
                          waits=(tmax(tok_cc[ccn]) if (t == 0 and c0 in (0, 4))
                                 else []),
                          inc=True, dsem=f"sa{j}")
            aot_dma[t] = tk

        sel_tok = {}

        def emit_sel(t):
            j = t % 2
            ts0 = emit("vector", lambda e, j=j: e.tensor_scalar(
                aot[j][:, :], aotl[j][:, :], sela[:, 0:1], None, ALU.mult),
                waits=tmax(aot_dma[t], aot_use[j]), inc=True)
            ts1 = emit("vector", lambda e, j=j: e.scalar_tensor_tensor(
                aot[j][:, :], aoth[j][:, :], selb[:, 0:1], aot[j][:, :],
                ALU.mult, ALU.add), waits=[ts0], inc=True)
            sel_use[j] = ts1
            sel_tok[t] = ts1

        aot_load(0)
        emit_sel(0)
        for t in range(8):
            j = t % 2
            taot = sel_tok[t]
            txr = emit("sync", dma(xrs[j][:, :], d_xres[t * 128:(t + 1) * 128, :]),
                       waits=[xrs_use[j]], inc=True, dsem=f"sx{j}")
            if t + 1 < 8:
                aot_load(t + 1)
            f_toks = []
            for f in range(4):
                nS, ptS, oS = REGS[rr % 4]
                rr += 1
                regS = ptS[:, oS:oS + 512]
                nG, ptG, oG = REGS[rr % 4]
                rr += 1
                regG = ptG[:, oG:oG + 512]
                tok = None
                for c in range(NC):
                    w = tmax(bank[nS], wo_tok[c], taot, t_bob) if c == 0 else [wo_tok[c]]
                    tok = emit("tensor", mm(
                        regS, aot[j][:, c * 128:(c + 1) * 128],
                        wo_b[:, c * E + f * 512:c * E + f * 512 + 512],
                        c == 0, False), waits=w, inc=False)
                t_s = emit("tensor", mm(regS, ones1b[0:1, :],
                                        borb[0:1, f * 512:(f + 1) * 512], False, True),
                           inc=True)
                w = tmax(bank[nG], t_wub, t_bub, t_wsel)
                emit("tensor", mm(regG, hpr_my[:, t * 128:(t + 1) * 128],
                                  wu_b[:, f * 512:(f + 1) * 512], True, False),
                     waits=w, inc=False)
                t_g = emit("tensor", mm(regG, wTb_my[0:8, t * 128:(t + 1) * 128],
                                        bu_b[0:8, f * 512:(f + 1) * 512], False, True),
                           inc=True)
                jf = f % 2
                t_sil = emit("scalar", lambda e, jf=jf, regG=regG: e.activation(
                    gsil[jf][:, :], regG, AF.Silu, bias=0.0, scale=1.0),
                    waits=tmax(t_g, gsil_use[jf]), inc=True)
                t_mu = emit("vector", lambda e, j=j, f=f, jf=jf, regS=regS: e.tensor_tensor(
                    outb[j][:, f * 512:(f + 1) * 512], regS, gsil[jf][:, :], ALU.mult),
                    waits=tmax(t_s, t_sil, outb_use[j]), inc=True)
                t_ad = emit("vector", lambda e, j=j, f=f: e.tensor_tensor(
                    outb[j][:, f * 512:(f + 1) * 512],
                    outb[j][:, f * 512:(f + 1) * 512],
                    xrs[j][:, f * 512:(f + 1) * 512], ALU.add),
                    waits=tmax(t_mu, txr), inc=True)
                f_toks.append(t_ad)
                bank[nS] = t_mu
                bank[nG] = t_sil
                gsil_use[jf] = t_mu
                aot_use[j] = t_s
                xrs_use[j] = t_ad
                if f == 1 and t + 1 < 8:
                    emit_sel(t + 1)
            t_out = emit("gpsimd", dma(d_out[t * 128:(t + 1) * 128, :], outb[j][:, :]),
                         waits=tmax(*f_toks), inc=True, dsem=f"go{j}")
            out_toks.append(t_out)
            outb_use[j] = t_out
        P.waits("gpsimd", [out_toks[-1], out_toks[-2]])
        P.flush_block()

    return nc


_CACHE = {}


def kernel(L0_attn_x, L0_attn_Wq, L0_attn_bq, L0_attn_Wk, L0_attn_bk,
           L0_attn_Wv, L0_attn_bv, L0_attn_Wo, L0_attn_bo,
           L0_ffn_Wr, L0_ffn_br, L0_ffn_Wd, L0_ffn_bd, L0_ffn_Wu, L0_ffn_bu):
    x = np.ascontiguousarray(np.asarray(L0_attn_x, dtype=np.float32))
    Wq = np.asarray(L0_attn_Wq, np.float32)
    Wk = np.asarray(L0_attn_Wk, np.float32)
    Wv = np.asarray(L0_attn_Wv, np.float32)
    Wo = np.asarray(L0_attn_Wo, np.float32)
    bq = np.asarray(L0_attn_bq, np.float32)
    bk = np.asarray(L0_attn_bk, np.float32)
    bv = np.asarray(L0_attn_bv, np.float32)
    bo = np.asarray(L0_attn_bo, np.float32)
    Wr = np.asarray(L0_ffn_Wr, np.float32)
    br = np.asarray(L0_ffn_br, np.float32)
    Wd = np.asarray(L0_ffn_Wd, np.float32)
    bd = np.asarray(L0_ffn_bd, np.float32)
    Wu = np.asarray(L0_ffn_Wu, np.float32)
    bu = np.asarray(L0_ffn_bu, np.float32)

    crow = _build_crow()
    alibi, lt, rep, id8, id128 = _host_consts()

    def chunk_major(wt):
        n = wt.shape[1]
        return np.ascontiguousarray(
            wt.reshape(NC, 128, n).transpose(1, 0, 2).reshape(128, NC * n))

    WqT = np.ascontiguousarray(Wq.T)
    WkT = np.ascontiguousarray(Wk.T)
    WvT = np.ascontiguousarray(Wv.T)
    WoT = np.ascontiguousarray(Wo.T)
    wq_h = WqT.reshape(NC, 128, H, D).transpose(1, 2, 0, 3)
    wk_h = WkT.reshape(NC, 128, H, D).transpose(1, 2, 0, 3)
    wv_h = WvT.reshape(NC, 128, 4, 512).transpose(1, 2, 0, 3)
    wo_h = chunk_major(WoT)
    w33 = np.zeros((NC, 128, 33), np.float32)
    w33[:, :, 0:8] = Wr.reshape(NC, 128, 8)
    w33[:, :, 32] = 1.0
    wr_h = np.ascontiguousarray(w33.transpose(1, 0, 2).reshape(128, NC * 33))
    Wd_flat = np.ascontiguousarray(Wd.transpose(1, 0, 2).reshape(E, 128))
    wd_h = chunk_major(Wd_flat)
    wu_h = np.ascontiguousarray(Wu.reshape(128, E))
    bu_h = np.ascontiguousarray(bu)
    bqc = bq.reshape(H, 128).T
    bkc = bk.reshape(H, 128).T

    shared = {
        "wo": wo_h, "bor": bo.reshape(1, E).copy(), "lt": lt,
        "wr": wr_h, "br": br.reshape(8, 1).copy(),
        "wd": wd_h, "bd": bd.reshape(128, 1).copy(),
        "wu": wu_h, "bu": bu_h,
        "repm": rep, "id8": id8, "id128": id128,
    }
    hgv = []
    for hg in range(2):
        hs = slice(hg * NH, (hg + 1) * NH)
        sel = np.float32(1.0) if hg == 0 else np.float32(0.0)
        hgv.append({
            "wq": np.ascontiguousarray(wq_h[:, hs].reshape(128, NH * NC * D)),
            "wk": np.ascontiguousarray(wk_h[:, hs].reshape(128, NH * NC * D)),
            "wv": np.ascontiguousarray(
                wv_h[:, hg * 2:(hg + 1) * 2].reshape(128, 2 * NC * 512)),
            "bqc": np.ascontiguousarray(bqc[:, hs]),
            "bkc": np.ascontiguousarray(bkc[:, hs]),
            "bvr": bv[hg * NH * D:(hg + 1) * NH * D].reshape(1, NH * D).copy(),
            "alibi": np.ascontiguousarray(alibi[:, hg * NH * 16:(hg + 1) * NH * 16]),
            "crow": np.ascontiguousarray(crow[:, hg * NH * S:(hg + 1) * NH * S]),
            "sela": np.full((128, 1), 1.0 - hg, np.float32),
            "selb": np.full((128, 1), float(hg), np.float32),
        })

    key = "v2"
    if key not in _CACHE:
        _CACHE[key] = build_nc()
    nc = _CACHE[key]

    in_maps = []
    for c in range(8):
        b, hg = c // 2, c % 2
        xT = np.ascontiguousarray(x[b].T)
        m = dict(shared)
        m.update(hgv[hg])
        m["xfm"] = np.ascontiguousarray(
            xT.reshape(NC, 128, S).transpose(1, 0, 2).reshape(128, NC * S))
        m["xres"] = np.ascontiguousarray(x[b, hg * QH:(hg + 1) * QH])
        in_maps.append(m)

    res = run_bass_kernel_spmd(nc, in_maps, core_ids=list(range(8)))
    out = np.empty((4, S, E), np.float32)
    for c in range(8):
        b, hg = c // 2, c % 2
        out[b, hg * QH:(hg + 1) * QH] = res.results[c]["out"]
    return out


# version 32
# speedup vs baseline: 1.0409x; 1.0409x over previous
import contextlib
import math

import numpy as np
import ml_dtypes

import concourse.bass as bass
import concourse.mybir as mybir
from concourse.bass_utils import run_bass_kernel_spmd

F32 = mybir.dt.float32
BF16 = mybir.dt.bfloat16
AF = mybir.ActivationFunctionType
ALU = mybir.AluOpType
AX = mybir.AxisListType

E = 2048
H = 16
NH = 8
D = 128
S = 2048
NC = 16
NT = 16
QH = 1024
INV_SQRT_D = 1.0 / math.sqrt(D)
OFFS = 8.0


def _slopes():
    return [2.0 ** (-0.5 * (i + 1)) for i in range(H)]


def _build_crow():
    sl = _slopes()
    qq = np.arange(S).astype(np.float64)
    rows = [(-math.sqrt(D) * sl[h] * qq).astype(np.float32) for h in range(H)]
    flat = np.concatenate(rows).astype(ml_dtypes.bfloat16)
    return flat.reshape(1, -1)


def _host_consts():
    sl = _slopes()
    alibi = np.zeros((128, H * 16), np.float32)
    p = np.arange(128, dtype=np.float64)
    for h in range(H):
        for kt in range(16):
            alibi[:, h * 16 + kt] = (sl[h] * (kt * 128 + p) + OFFS).astype(np.float32)
    lt = np.where(np.arange(128)[:, None] <= np.arange(128)[None, :],
                  np.float32(0.0), np.float32(-1e9)).astype(np.float32)
    rep = np.zeros((8, 128), np.float32)
    for e_ in range(8):
        rep[e_, e_ * 16:(e_ + 1) * 16] = 1.0
    rep = rep.astype(ml_dtypes.bfloat16)
    return alibi, lt, rep, np.eye(8, dtype=np.float32), np.eye(128, dtype=np.float32)


class Prog:
    ENGS = ("sync", "tensor", "scalar", "vector", "gpsimd")
    SEM = {"tensor": ("pe", 1), "scalar": ("act", 1), "vector": ("dve", 1)}
    DSEMS = ("sdc", "sd0", "sd1", "sc0", "sc1", "ga0", "ga1", "wo0", "wo1",
             "sa0", "sa1", "sx0", "sx1", "go0", "go1", "ccA", "ccB1", "ccB1b", "ccB2")

    def __init__(self, nc, ctx):
        self.nc = nc
        self.thunks = {e: [] for e in self.ENGS}
        self.sems = {}
        self.cnt = {}
        for name in ("pe", "act", "dve") + self.DSEMS:
            self.sems[name] = ctx.enter_context(nc.semaphore(name))
            self.cnt[name] = 0

    def emit(self, eng, make, waits=(), inc=False, dsem=None):
        wl = []
        for t in waits:
            if t is None:
                continue
            s, v = t
            wl.append((self.sems[s], v))
        tok = None
        incs = None
        if inc:
            if eng in ("sync", "gpsimd"):
                assert dsem is not None
                sname, by = dsem, (1 if dsem.startswith("cc") else 16)
            else:
                sname, by = self.SEM[eng]
            self.cnt[sname] += by
            tok = (sname, self.cnt[sname])
            incs = (self.sems[sname], by)

        def thunk(e, make=make, wl=wl, incs=incs):
            for (s, v) in wl:
                e.wait_ge(s, v)
            ins = make(e)
            if incs is not None:
                ins.then_inc(incs[0], incs[1])

        self.thunks[eng].append(thunk)
        return tok

    def waits(self, eng, waits):
        wl = [(self.sems[s], v) for t in waits if t is not None for (s, v) in [t]]
        if not wl:
            return

        def thunk(e, wl=wl):
            for (s, v) in wl:
                e.wait_ge(s, v)

        self.thunks[eng].append(thunk)

    def flush_block(self):
        thunks = self.thunks
        self.thunks = {e: [] for e in self.ENGS}
        with self.nc.Block() as blk:
            for eng in self.ENGS:
                if thunks[eng]:
                    def runner(e, lst=thunks[eng]):
                        for t in lst:
                            t(e)
                    getattr(blk, eng)(runner)


def tmax(*toks):
    best = {}
    for t in toks:
        if t is None:
            continue
        s, v = t
        if s not in best or best[s] < v:
            best[s] = v
    return [(s, v) for s, v in best.items()]


def build_nc():
    nc = bass.Bass("TRN2", target_bir_lowering=False)

    d_xfm = nc.dram_tensor("xfm", [128, NC * S], F32, kind="ExternalInput")
    d_xres = nc.dram_tensor("xres", [QH, E], F32, kind="ExternalInput")
    d_wq = nc.dram_tensor("wq", [128, NH * NC * D], F32, kind="ExternalInput")
    d_wk = nc.dram_tensor("wk", [128, NH * NC * D], F32, kind="ExternalInput")
    d_wv = nc.dram_tensor("wv", [128, 2 * NC * 512], F32, kind="ExternalInput")
    d_wo = nc.dram_tensor("wo", [128, NC * E], F32, kind="ExternalInput")
    d_bqc = nc.dram_tensor("bqc", [128, NH], F32, kind="ExternalInput")
    d_bkc = nc.dram_tensor("bkc", [128, NH], F32, kind="ExternalInput")
    d_bvr = nc.dram_tensor("bvr", [1, NH * D], F32, kind="ExternalInput")
    d_bor = nc.dram_tensor("bor", [1, E], F32, kind="ExternalInput")
    d_wr = nc.dram_tensor("wr", [128, NC * 33], F32, kind="ExternalInput")
    d_br = nc.dram_tensor("br", [8, 1], F32, kind="ExternalInput")
    d_wd = nc.dram_tensor("wd", [128, NC * 128], F32, kind="ExternalInput")
    d_bd = nc.dram_tensor("bd", [128, 1], F32, kind="ExternalInput")
    d_wu = nc.dram_tensor("wu", [128, E], F32, kind="ExternalInput")
    d_bu = nc.dram_tensor("bu", [8, E], F32, kind="ExternalInput")
    d_alibi = nc.dram_tensor("alibi", [128, NH * 16], F32, kind="ExternalInput")
    d_lt = nc.dram_tensor("lt", [128, 128], F32, kind="ExternalInput")
    d_rep = nc.dram_tensor("repm", [8, 128], BF16, kind="ExternalInput")
    d_id8 = nc.dram_tensor("id8", [8, 8], F32, kind="ExternalInput")
    d_id128 = nc.dram_tensor("id128", [128, 128], F32, kind="ExternalInput")
    d_crow = nc.dram_tensor("crow", [1, NH * S], BF16, kind="ExternalInput")
    d_sela = nc.dram_tensor("sela", [128, 1], F32, kind="ExternalInput")
    d_selb = nc.dram_tensor("selb", [128, 1], F32, kind="ExternalInput")
    d_out = nc.dram_tensor("out", [QH, E], F32, kind="ExternalOutput")
    d_aoscr = [nc.dram_tensor("aoscr0", [NH * D, QH], BF16),
               nc.dram_tensor("aoscr1a", [3 * D, QH], BF16),
               nc.dram_tensor("aoscr1b", [4 * D, QH], BF16),
               nc.dram_tensor("aoscr1c", [D, QH], BF16)]
    d_hscr = nc.dram_tensor("hscr", [128, S], BF16)
    d_wtscr = nc.dram_tensor("wtscr", [8, S], BF16)
    d_ccout = [nc.dram_tensor("ccout0", [E, QH], BF16),
               nc.dram_tensor("ccout1a", [6 * D, QH], BF16),
               nc.dram_tensor("ccout1b", [NH * D, QH], BF16),
               nc.dram_tensor("ccout1c", [2 * D, QH], BF16)]

    with contextlib.ExitStack() as ctx:
        en = ctx.enter_context
        P = Prog(nc, ctx)
        emit = P.emit

        stage = [en(nc.sbuf_tensor(f"stage{i}", [128, 2048], F32)) for i in range(2)]
        ones1b = en(nc.sbuf_tensor("ones1b", [1, 128], BF16))
        ones1f = en(nc.sbuf_tensor("ones1f", [1, 128], F32))
        onescb = en(nc.sbuf_tensor("onescb", [128, 128], BF16))
        epsc = en(nc.sbuf_tensor("epsc", [1, 1], F32))
        alibi = en(nc.sbuf_tensor("alibi_s", [128, NH * 16], F32))
        lowtri = en(nc.sbuf_tensor("lowtri", [128, 128], F32))
        bqc = en(nc.sbuf_tensor("bqc_s", [128, NH], F32))
        bkc = en(nc.sbuf_tensor("bkc_s", [128, NH], F32))
        bvrb = en(nc.sbuf_tensor("bvrb", [1, NH * D], BF16))
        sela = en(nc.sbuf_tensor("sela_s", [128, 1], F32))
        selb = en(nc.sbuf_tensor("selb_s", [128, 1], F32))

        psA = en(nc.psum_tensor("psA", [128, QH], F32))
        psB = en(nc.psum_tensor("psB", [128, QH], F32))
        psPV = en(nc.psum_tensor("psPV", [128, QH], F32))
        psD1 = en(nc.psum_tensor("psD1", [128, 512], F32))
        psD2 = en(nc.psum_tensor("psD2", [128, 512], F32))
        SC = [psA, psB]
        REGS = [("A0", psA, 0), ("B0", psB, 0), ("A1", psA, 512), ("B1", psB, 512)]
        bank = {}

        def mm(out_ap, lhsT_ap, rhs_ap, start, stop):
            return lambda e: e.matmul(out_ap, lhsT_ap, rhs_ap, start=start, stop=stop)

        def dma(dst, src):
            return lambda e: e.dma_start(dst, src)

        ctxm = contextlib.ExitStack()
        em = ctxm.enter_context
        wrf = em(nc.sbuf_tensor("wrf", [128, NC * 33], F32))
        onescf = em(nc.sbuf_tensor("onescf", [128, 1], F32))
        wsumr = em(nc.sbuf_tensor("wsumr", [8, 1], F32))
        brp = em(nc.sbuf_tensor("brp", [8, 1], F32))
        brpb = em(nc.sbuf_tensor("brpb", [8, 1], F32))
        Vg = [en(nc.sbuf_tensor("Vg0", [128, 8192], BF16, side="right"))]
        ctx_x = contextlib.ExitStack()
        ex = ctx_x.enter_context
        xbf = [ex(nc.sbuf_tensor(f"xbf{c}", [128, S], BF16, side="right"))
               for c in range(NC)]
        wkq_b = [ex(nc.sbuf_tensor(f"wkq_b{i}", [128, 2048], BF16, side="right"))
                 for i in range(2)]
        wv_b = ex(nc.sbuf_tensor("wv_b", [128, 8192], BF16, side="right"))
        ctx_lg = contextlib.ExitStack()
        lgraw = ctx_lg.enter_context(nc.sbuf_tensor("lgraw", [8, S], F32))
        lgfin = ctx_lg.enter_context(nc.sbuf_tensor("lgfin", [8, S], F32))

        for dst, src in [
            (alibi[:, :], d_alibi[:, :]), (lowtri[:, :], d_lt[:, :]),
            (bqc[:, :], d_bqc[:, :]), (bkc[:, :], d_bkc[:, :]),
            (stage[0][0:1, 0:NH * D], d_bvr[:, :]), (wrf[:, :], d_wr[:, :]),
            (sela[:, :], d_sela[:, :]), (selb[:, :], d_selb[:, :]),
        ]:
            emit("sync", dma(dst, src), inc=True, dsem="sdc")
        tok_c0 = emit("sync", dma(brp[:, :], d_br[:, :]), inc=True, dsem="sdc")
        for tens, val in [(ones1b, 1.0), (ones1f, 1.0), (onescb, 1.0),
                          (onescf, 1.0), (epsc, 1e-5)]:
            emit("vector", lambda e, t=tens, v=val: e.memset(t[:, :], v), inc=False)
        emit("vector", lambda e: e.tensor_copy(bvrb[:, :], stage[0][0:1, 0:NH * D]),
             waits=[tok_c0], inc=False)
        emit("vector", lambda e: e.tensor_scalar(brpb[:, :], brp[:, :], 2.9994, None, ALU.add),
             waits=[tok_c0], inc=True)
        P.waits("vector", [tok_c0])
        P.flush_block()

        LREG = [psA[0:33, 0:512], psA[0:33, 512:QH],
                psB[0:33, 0:512], psB[0:33, 512:QH]]
        LG8 = [psA[0:8, 0:512], psA[0:8, 512:QH],
               psB[0:8, 0:512], psB[0:8, 512:QH]]
        STAT_XSQ = [psPV[0:8, 0:512], psPV[0:8, 512:QH],
                    psD1[0:8, :], psD2[0:8, :]]
        ctx_sq = contextlib.ExitStack()
        xsqb = [ctx_sq.enter_context(nc.sbuf_tensor(f"xsq{i}", [128, 512], BF16))
                for i in range(4)]
        sq_guard = [None] * 4
        last_cast = [None, None]
        last_lgmm = [None, None]
        for c in range(NC):
            j = c % 2
            td = emit("sync", dma(stage[j][:, :], d_xfm[:, c * S:(c + 1) * S]),
                      waits=tmax(last_cast[j], last_lgmm[j]), inc=True, dsem=f"sd{j}")
            last_cast[j] = emit(
                "vector", lambda e, c=c, j=j: e.tensor_copy(xbf[c][:, :], stage[j][:, :]),
                waits=[td], inc=True)
            for tcn in range(4):
                last_lgmm[j] = emit("tensor", mm(
                    LREG[tcn], wrf[:, c * 33:(c + 1) * 33],
                    stage[j][:, tcn * 512:(tcn + 1) * 512],
                    c == 0, c == NC - 1),
                    waits=(tmax(td, tok_c0) if tcn == 0 else [td]),
                    inc=(tcn == 3))
            for tcn in range(4):
                tsq = emit("vector", lambda e, c=c, tcn=tcn: e.tensor_mul(
                    xsqb[tcn][:, :], xbf[c][:, tcn * 512:(tcn + 1) * 512],
                    xbf[c][:, tcn * 512:(tcn + 1) * 512]),
                    waits=tmax(last_cast[j], sq_guard[tcn]), inc=True)
                sq_guard[tcn] = emit("tensor", mm(
                    STAT_XSQ[tcn], onescb[:, 0:8], xsqb[tcn][:, :],
                    c == 0, c == NC - 1), waits=[tsq], inc=True)
        stats_done = sq_guard[3]
        for tcn in range(4):
            emit("vector", lambda e, tcn=tcn: e.tensor_copy(
                lgraw[0:8, tcn * 512:(tcn + 1) * 512], LG8[tcn]),
                waits=[last_lgmm[1]] if tcn == 0 else [], inc=False)
        P.flush_block()

        ctx_sq.close()
        ctx1 = contextlib.ExitStack()
        if True:
            e1 = ctx1.enter_context
            a_rep = e1(nc.sbuf_tensor("a_rep", [128, S], F32))
            ma_rep = e1(nc.sbuf_tensor("ma_rep", [128, S], F32))
            arow = e1(nc.sbuf_tensor("arow", [1, S], F32))
            marow = e1(nc.sbuf_tensor("marow", [1, S], F32))
            trow = e1(nc.sbuf_tensor("trow", [1, 512], F32))
            t2row = e1(nc.sbuf_tensor("t2row", [1, 512], F32))
            t3row = e1(nc.sbuf_tensor("t3row", [1, 512], F32))
            SROW = [(psA[32:33, 0:512], psPV[0:1, 0:512]),
                    (psA[32:33, 512:QH], psPV[0:1, 512:QH]),
                    (psB[32:33, 0:512], psD1[0:1, :]),
                    (psB[32:33, 512:QH], psD2[0:1, :])]
            t_wsum = None
            for c in range(NC):
                t_wsum = emit("tensor", mm(
                    psD2[32:40, 0:1], wrf[:, c * 33:c * 33 + 8],
                    onescf[:, 0:1], c == 0, c == NC - 1),
                    inc=(c == NC - 1))
            t_wsum_c = emit("vector", lambda e: e.tensor_copy(
                wsumr[:, :], psD2[32:40, 0:1]), waits=[t_wsum], inc=True)
            rep_toks = []
            prev_r5 = None
            prev_r6 = None
            prev_r7 = None
            for tcn in range(4):
                cs = slice(tcn * 512, (tcn + 1) * 512)
                xs_r, xq_r = SROW[tcn]
                r1 = emit("vector", lambda e, xs_r=xs_r: e.tensor_scalar(
                    trow[:, :], xs_r, 1.0 / E, None, ALU.mult),
                    waits=(tmax(stats_done) if tcn == 0 else tmax(prev_r7)),
                    inc=True)
                r2 = emit("vector", lambda e, xq_r=xq_r: e.tensor_scalar(
                    t2row[:, :], xq_r, 1.0 / E, None, ALU.mult),
                    waits=tmax(r1, prev_r5), inc=True)
                r3 = emit("vector", lambda e: e.tensor_mul(t3row[:, :], trow[:, :], trow[:, :]),
                          waits=tmax(r2, prev_r6), inc=True)
                r4 = emit("vector", lambda e: e.tensor_sub(t2row[:, :], t2row[:, :], t3row[:, :]),
                          waits=[r3], inc=True)
                r5 = emit("scalar", lambda e: e.activation(
                    t3row[:, :], t2row[:, :], AF.Ln, bias=epsc[0:1, 0:1], scale=1.0),
                    waits=[r4], inc=True)
                r6 = emit("scalar", lambda e, cs=cs: e.activation(
                    arow[0:1, cs], t3row[:, :], AF.Exp, bias=0.0, scale=-0.5),
                    waits=[r5], inc=True)
                r7 = emit("vector", lambda e, cs=cs: e.tensor_mul(
                    marow[0:1, cs], trow[:, :], arow[0:1, cs]),
                    waits=[r6], inc=True)
                prev_r5, prev_r6, prev_r7 = r5, r6, r7
                rep_toks.append((r6, r7))
            REP_A = [psA[:, 0:512], psA[:, 512:QH], psB[:, 0:512], psB[:, 512:QH]]
            REP_B = [psPV[:, 0:512], psPV[:, 512:QH], psD1[:, :], psD2[:, :]]
            tca_l = []
            last_rep = None
            for tcn in range(4):
                cs = slice(tcn * 512, (tcn + 1) * 512)
                ta = emit("tensor", mm(REP_A[tcn], ones1f[0:1, :], arow[0:1, cs],
                                       True, True),
                          waits=tmax(rep_toks[tcn][0]), inc=True)
                tca = emit("vector", lambda e, cs=cs, tcn=tcn: e.tensor_copy(
                    a_rep[:, cs], REP_A[tcn]), waits=[ta], inc=True)
                tca_l.append(tca)
                tb = emit("tensor", mm(REP_B[tcn], ones1f[0:1, :], marow[0:1, cs],
                                       True, True),
                          waits=tmax(rep_toks[tcn][1],
                                     (t_wsum_c if tcn == 3 else None)), inc=True)
                tcb = emit("vector", lambda e, cs=cs, tcn=tcn: e.tensor_copy(
                    ma_rep[:, cs], REP_B[tcn]), waits=[tb], inc=True)
                last_rep = tcb
            lg2 = e1(nc.sbuf_tensor("lg2", [8, S], F32))
            f1 = emit("vector", lambda e: e.tensor_mul(lgfin[:, :], lgraw[:, :], a_rep[0:8, :]),
                      waits=[last_rep], inc=True)
            f2 = emit("vector", lambda e: e.scalar_tensor_tensor(
                lg2[:, :], ma_rep[0:8, :], wsumr[0:8, 0:1], lgfin[:, :],
                ALU.mult, ALU.subtract), waits=tmax(f1, t_wsum_c), inc=True)
            f3 = emit("vector", lambda e: e.tensor_scalar(
                lgfin[:, :], lg2[:, :], -1.0, brpb[0:8, 0:1], ALU.mult, ALU.add),
                waits=[f2], inc=True)

        ctx2 = contextlib.ExitStack()
        e2 = ctx2.enter_context
        repm = e2(nc.sbuf_tensor("repm_s", [8, 128], BF16))
        id8 = e2(nc.sbuf_tensor("id8_s", [8, 8], F32))
        id128 = e2(nc.sbuf_tensor("id128_s", [128, 128], F32))
        bdc = e2(nc.sbuf_tensor("bdc", [128, 1], F32))
        wd_b = e2(nc.sbuf_tensor("wd_b", [128, NC * 128], BF16))
        hpr = e2(nc.sbuf_tensor("hpr", [128, S], BF16))
        wTb = e2(nc.sbuf_tensor("wTb", [8, S], BF16))
        gtok = e2(nc.sbuf_tensor("gtok", [128, 128], F32))
        wtok = e2(nc.sbuf_tensor("wtok", [128, 128], F32))
        g_m1 = e2(nc.sbuf_tensor("g_m1", [128, 16], F32))
        g_m2 = e2(nc.sbuf_tensor("g_m2", [128, 16], F32))
        g_k1 = e2(nc.sbuf_tensor("g_k1", [128, 128], F32))
        g_k2 = e2(nc.sbuf_tensor("g_k2", [128, 128], F32))
        g_sd = e2(nc.sbuf_tensor("g_sd", [128, 16], F32))
        g_sd1 = e2(nc.sbuf_tensor("g_sd1", [128, 16], F32))
        g_gg = e2(nc.sbuf_tensor("g_gg", [128, 128], F32))
        for dst, src in [
            (repm[:, :], d_rep[:, :]), (id8[:, :], d_id8[:, :]),
            (id128[:, :], d_id128[:, :]), (bdc[:, :], d_bd[:, :]),
        ]:
            ldt = emit("sync", dma(dst, src), inc=True, dsem="sdc")
        t_wd = emit("sync", dma(stage[0][:, :], d_wd[:, :]), inc=True, dsem="sd0")
        t_wdc = emit("vector", lambda e: e.tensor_copy(wd_b[:, :], stage[0][:, :]),
                     waits=[t_wd], inc=True)
        stage_use = [t_wdc, None]
        vload_tok = {}

        def v_load(g, wv_guard):
            tcv = None
            for j4 in range(4):
                jj = j4 % 2
                tdv = emit("sync", dma(stage[jj][:, :],
                                       d_wv[:, (g * 4 + j4) * 2048:(g * 4 + j4 + 1) * 2048]),
                           waits=[stage_use[jj]], inc=True, dsem=f"sd{jj}")
                tcv = emit("scalar", lambda e, j4=j4, jj=jj: e.copy(
                    wv_b[:, j4 * 2048:(j4 + 1) * 2048], stage[jj][:, :]),
                    waits=tmax(tdv, (wv_guard if j4 == 0 else None)), inc=True)
                stage_use[jj] = tcv
            vload_tok[g] = tcv

        v_load(0, None)
        bank["A0"] = tca_l[0]
        bank["A1"] = tca_l[1]
        bank["B0"] = tca_l[2]
        bank["B1"] = tca_l[3]
        rr = 0

        def reg_next():
            nonlocal rr
            name, t, o = REGS[rr % 4]
            rr += 1
            return name, t, o

        t_gt = None
        for t in range(NT):
            name, pt, o = reg_next()
            reg = pt[:, o:o + 8]
            ttr = emit("tensor", lambda e, t=t, reg=reg: e.transpose(
                reg, lgfin[0:8, t * 128:(t + 1) * 128], id8[:, :]),
                waits=tmax(bank[name], ldt, f3), inc=True)
            t_gt = emit("vector", lambda e, t=t, reg=reg: e.tensor_copy(
                gtok[:, t * 8:(t + 1) * 8], reg), waits=[ttr], inc=True)
            bank[name] = t_gt

        def b3(tens):
            return bass.AP(tens, 0, [[128, 128], [8, 16], [1, 8]])

        def b3out(tens):
            return bass.AP(tens, 0, [[16, 128], [1, 16], [1, 1]])

        def bc3(tens):
            return bass.AP(tens, 0, [[16, 128], [1, 16], [0, 8]])

        q1 = emit("vector", lambda e: e.tensor_reduce(b3out(g_m1), b3(gtok), AX.X, ALU.max),
                  waits=[t_gt], inc=True)
        q2 = emit("vector", lambda e: e.tensor_tensor(b3(g_k1), b3(gtok), bc3(g_m1), ALU.is_ge),
                  waits=[q1], inc=True)
        q3 = emit("vector", lambda e: e.scalar_tensor_tensor(
            b3(g_gg), b3(g_k1), -1e9, b3(gtok), ALU.mult, ALU.add), waits=[q2], inc=True)
        q4 = emit("vector", lambda e: e.tensor_reduce(b3out(g_m2), b3(g_gg), AX.X, ALU.max),
                  waits=[q3], inc=True)
        q5 = emit("vector", lambda e: e.tensor_tensor(b3(g_k2), b3(g_gg), bc3(g_m2), ALU.is_ge),
                  waits=[q4], inc=True)
        q6 = emit("vector", lambda e: e.tensor_sub(g_sd1[:, :], g_m2[:, :], g_m1[:, :]),
                  waits=[q5], inc=True)
        q7 = emit("scalar", lambda e: e.activation(g_sd[:, :], g_sd1[:, :], AF.Sigmoid,
                                                   bias=0.0, scale=1.0), waits=[q6], inc=True)
        q8 = emit("vector", lambda e: e.tensor_scalar(g_sd1[:, :], g_sd[:, :], -1.0, 1.0,
                                                      ALU.mult, ALU.add), waits=[q7], inc=True)
        q9 = emit("vector", lambda e: e.tensor_tensor(wtok[:, :], g_k1[:, :], bc3(g_sd1), ALU.mult),
                  waits=[q8], inc=True)
        qa = emit("vector", lambda e: e.tensor_tensor(g_gg[:, :], g_k2[:, :], bc3(g_sd), ALU.mult),
                  waits=[q9], inc=True)
        qb = emit("vector", lambda e: e.tensor_add(wtok[:, :], wtok[:, :], g_gg[:, :]),
                  waits=[qa], inc=True)
        t_wt = None
        for t in range(NT):
            name, pt, o = reg_next()
            reg = pt[0:8, o:o + 128]
            ttr = emit("tensor", lambda e, t=t, reg=reg: e.transpose(
                reg, wtok[:, t * 8:(t + 1) * 8], id128[:, :]),
                waits=tmax(bank[name], qb), inc=True)
            t_wt = emit("vector", lambda e, t=t, reg=reg: e.tensor_copy(
                wTb[0:8, t * 128:(t + 1) * 128], reg), waits=[ttr], inc=True)
            bank[name] = t_wt
        t_wtb = t_wt
        xn_tok = {}
        for c in range(NC):
            x1 = emit("vector", lambda e, c=c: e.tensor_mul(
                xbf[c][:, :], xbf[c][:, :], a_rep[:, :]),
                waits=[last_rep] if c == 0 else [], inc=True)
            xn_tok[c] = emit("vector", lambda e, c=c: e.tensor_sub(
                xbf[c][:, :], xbf[c][:, :], ma_rep[:, :]), waits=[x1], inc=True)
        HREGS = [psPV[:, 0:512], psPV[:, 512:QH], psD1[:, :], psD2[:, :]]
        rtail = last_rep
        tw_toks = []
        for tcn in range(4):
            nW, ptW, oW = reg_next()
            regW = ptW[:, oW:oW + 512]
            tw = emit("tensor", mm(regW, repm[0:8, :],
                                   bass.AP(wTb, tcn * 512, [[S, 8], [1, 512]]), True, True),
                      waits=tmax(bank[nW], t_wtb, ldt), inc=True)
            tw_toks.append((nW, regW, tw))
        htok = [None] * 4
        for c in range(NC):
            for tcn in range(4):
                w = []
                if tcn == 0:
                    w = tmax(xn_tok[c], t_wdc, (rtail if c == 0 else None))
                htok[tcn] = emit("tensor", mm(
                    HREGS[tcn], wd_b[:, c * 128:(c + 1) * 128],
                    bass.AP(xbf[c], tcn * 512, [[S, 128], [1, 512]]),
                    c == 0, c == NC - 1), waits=w, inc=(c == NC - 1))
        for tcn in range(4):
            cs = slice(tcn * 512, (tcn + 1) * 512)
            nW, regW, tw = tw_toks[tcn]
            j = tcn % 2
            trl = emit("scalar", lambda e, j=j, tcn=tcn: e.activation(
                stage[j][:, 0:512], HREGS[tcn], AF.Relu, bias=bdc[:, 0:1], scale=1.0),
                waits=tmax(htok[tcn], stage_use[j]), inc=True)
            thp = emit("vector", lambda e, cs=cs, j=j, regW=regW: e.tensor_tensor(
                hpr[:, cs], regW, stage[j][:, 0:512], ALU.mult),
                waits=[trl, tw], inc=True)
            stage_use[j] = thp
            bank[nW] = thp
        for t in range(NT):
            name, pt, o = reg_next()
            reg = pt[:, o:o + 512]
            tok = None
            for c in range(NC):
                w = tmax(bank[name], vload_tok[0]) if c == 0 else []
                tok = emit("tensor", mm(reg, xbf[c][:, t * 128:(t + 1) * 128],
                                        wv_b[:, c * 512:(c + 1) * 512],
                                        c == 0, False), waits=w, inc=False)
            tok = emit("tensor", mm(reg, ones1b[0:1, :],
                                    bvrb[0:1, 0:512], False, True), inc=True)
            bank[name] = emit("vector", lambda e, t=t, reg=reg: e.tensor_copy(
                Vg[0][:, t * 512:(t + 1) * 512], reg), waits=[tok], inc=True)
        th1 = emit("sync", dma(d_hscr[:, :], hpr[:, :]), waits=[thp], inc=True,
                   dsem="ga0")
        th2 = emit("sync", dma(d_wtscr[:, :], wTb[:, :]), waits=[t_wtb], inc=True,
                   dsem="ga1")
        P.waits("sync", [th1, th2])
        P.flush_block()
        ctx2.close()
        ctx1.close()
        ctx_lg.close()

        Kh = [em(nc.sbuf_tensor(f"Kh{i}", [128, S], BF16)) for i in range(NH)]
        Qh = [em(nc.sbuf_tensor(f"Qh{i}", [128, S], BF16)) for i in range(NH)]
        Vg.append(em(nc.sbuf_tensor("Vg1", [128, 8192], BF16)))
        for name, *_ in REGS:
            bank[name] = None
        stage_use = [None, None]
        wkq_use = [None, None]
        rr = 0
        tcv = None
        for j4 in range(4):
            jj = j4 % 2
            tdv = emit("sync", dma(stage[jj][:, :],
                                   d_wv[:, (4 + j4) * 2048:(5 + j4) * 2048]),
                       waits=[stage_use[jj]], inc=True, dsem=f"sd{jj}")
            tcv = emit("scalar", lambda e, j4=j4, jj=jj: e.copy(
                wv_b[:, j4 * 2048:(j4 + 1) * 2048], stage[jj][:, :]),
                waits=[tdv], inc=True)
            stage_use[jj] = tcv
        for t in range(NT):
            name, pt, o = REGS[rr % 4]
            rr += 1
            reg = pt[:, o:o + 512]
            tok = None
            for c in range(NC):
                w = tmax(bank[name], tcv) if c == 0 else []
                tok = emit("tensor", mm(reg, xbf[c][:, t * 128:(t + 1) * 128],
                                        wv_b[:, c * 512:(c + 1) * 512],
                                        c == 0, False), waits=w, inc=False)
            tok = emit("tensor", mm(reg, ones1b[0:1, :],
                                    bvrb[0:1, 512:1024], False, True), inc=True)
            bank[name] = emit("vector", lambda e, t=t, reg=reg: e.tensor_copy(
                Vg[1][:, t * 512:(t + 1) * 512], reg), waits=[tok], inc=True)
        for g in range(2):
            for hl in range(4):
                h = g * 4 + hl
                for wi, (which, dsrc, bias_col, dest) in enumerate((
                        ("k", d_wk, bkc, Kh), ("q", d_wq, bqc, Qh))):
                    jj = (hl * 2 + wi) % 2
                    tdw = emit("sync", dma(stage[jj][:, :],
                                           dsrc[:, h * 2048:(h + 1) * 2048]),
                               waits=[stage_use[jj]], inc=True, dsem=f"sd{jj}")
                    tcw = emit("scalar", lambda e, jj=jj: e.copy(
                        wkq_b[jj][:, :], stage[jj][:, :]),
                        waits=tmax(tdw, wkq_use[jj]), inc=True)
                    stage_use[jj] = tcw
                    for tcn in range(4):
                        name, pt, o = REGS[rr % 4]
                        rr += 1
                        reg = pt[:, o:o + 512]
                        tok = None
                        for c in range(NC):
                            w = tmax(bank[name], tcw) if c == 0 else []
                            tok = emit("tensor", mm(
                                reg, wkq_b[jj][:, c * 128:(c + 1) * 128],
                                bass.AP(xbf[c], tcn * 512, [[S, 128], [1, 512]]),
                                c == 0, c == NC - 1), waits=w, inc=(c == NC - 1))
                        wkq_use[jj] = tok
                        bank[name] = emit("vector", lambda e, tcn=tcn, reg=reg, h=h,
                                          bias_col=bias_col, dest=dest: e.tensor_scalar(
                            dest[h][:, tcn * 512:(tcn + 1) * 512], reg,
                            bias_col[:, h:h + 1], None, ALU.add),
                            waits=[tok], inc=True)
        P.flush_block()
        ctx_x.close()
        wo_b = en(nc.sbuf_tensor("wo_b", [128, NC * E], BF16, side="right"))
        expT = [em(nc.sbuf_tensor(f"expT{i}", [128, QH], BF16)) for i in range(2)]
        rep_sb = em(nc.sbuf_tensor("rep_sb", [128, QH], F32))
        ao_sb = [em(nc.sbuf_tensor(f"ao{i}", [128, QH], BF16)) for i in range(2)]
        crow_sb = [em(nc.sbuf_tensor(f"crow{i}", [1, S], BF16)) for i in range(2)]
        rep_use = (None, None)

        for name, *_ in REGS:
            bank[name] = None
        bank["SCf"] = [None, None]
        bank["PV"] = None
        bank["D1"] = None
        bank["D2"] = None
        ai = 0
        ao_store = {}
        crow_use = [None, None]
        ci = 0
        tok_cc = {}
        wo_dma = [None] * NC
        wo_tok = [None] * NC

        def wo_load(c):
            j = c % 2
            wo_dma[c] = emit("gpsimd", dma(stage[j][:, :],
                                           d_wo[:, c * E:(c + 1) * E]),
                             waits=[wo_tok[c - 2]] if c >= 2 else [],
                             inc=True, dsem=f"wo{j}")

        def wo_cast(c):
            j = c % 2
            wo_tok[c] = emit("vector", lambda e, c=c, j=j: e.tensor_copy(
                wo_b[:, c * E:(c + 1) * E], stage[j][:, :]),
                waits=[wo_dma[c]], inc=True)

        wo_load(0)
        wo_load(1)
        sched = [(0, hh) for hh in range(NH)] + \
                [(1, hh) for hh in (4, 5, 6, 7, 0, 1, 2, 3)]
        tdc_toks = {0: emit("sync", dma(crow_sb[0][0:1, 0:S],
                                        d_crow[0:1, sched[0][1] * S:(sched[0][1] + 1) * S]),
                            inc=True, dsem="sc0")}
        for ii, (qh, h) in enumerate(sched):
            Kt = 8 if qh == 0 else 16
            q0 = qh * QH
            if True:
                g, hl = h // 4, h % 4
                cb = ii % 2
                tdc = tdc_toks[ii]
                exp_toks = {}
                eff_toks = {}
                pv_toks = {}

                def pieces(qs):
                    if qs < 512:
                        return [(qs, 512 - qs), (512, 512)]
                    return [(qs, QH - qs)]

                def pieces_acc(kt, Kt=Kt, q0=q0):
                    qs = max(kt * 128 - q0, 0)
                    lk0 = (q0 + 512) // 128 - 1
                    lk1 = Kt - 1
                    segs = []
                    if kt * 128 >= q0:
                        segs.append([qs, 128])
                        rest = qs + 128
                    else:
                        rest = qs
                    while rest < QH:
                        end = 512 if rest < 512 else QH
                        segs.append([rest, end - rest])
                        rest = end
                    out = []
                    seen = set()
                    for off, nn in segs:
                        bk = 0 if off < 512 else 1
                        start = (kt == 0) and bk not in seen
                        seen.add(bk)
                        stop = (kt == (lk0 if bk == 0 else lk1))
                        out.append((off, nn, start, stop))
                    return out

                def emit_pv(kt, Kt=Kt, q0=q0, g=g, hl=hl, eff_toks=eff_toks,
                            pv_toks=pv_toks, pieces_acc=pieces_acc):
                    w0 = tmax(eff_toks[kt],
                              *( [bank["PV"], bank["D1"], bank["D2"]]
                                 if kt == 0 else []))
                    tok = None
                    pcs = pieces_acc(kt)
                    for idx, (off, nn, st, stp) in enumerate(pcs):
                        tok = emit("tensor", mm(
                            psPV[:, off:off + nn],
                            Vg[g][:, kt * 512 + hl * 128:kt * 512 + hl * 128 + 128],
                            expT[kt % 2][:, off:off + nn],
                            st, stp),
                            waits=(w0 if idx == 0 else []), inc=False)
                    for idx, (off, nn, st, stp) in enumerate(pcs):
                        if off < 512:
                            dreg = psD1[:, off:off + nn]
                        else:
                            dreg = psD2[:, off - 512:off - 512 + nn]
                        tok = emit("tensor", mm(
                            dreg, onescb[:, :], expT[kt % 2][:, off:off + nn],
                            st, stp), inc=(idx == len(pcs) - 1))
                    pv_toks[kt] = tok
                    return tok

                for kt in range(Kt):
                    qs = max(kt * 128 - q0, 0)
                    sc = SC[kt % 2]
                    scn = "A" if kt % 2 == 0 else "B"
                    w0 = tmax(exp_toks.get(kt - 2),
                              bank["SCf"][kt % 2],
                              *( [bank[scn + "0"], bank[scn + "1"], tdc]
                                 if kt < 2 else []))
                    for idx, (off, nn) in enumerate(pieces(qs)):
                        emit("tensor", mm(
                            sc[:, off:off + nn],
                            Kh[h][:, kt * 128:(kt + 1) * 128],
                            Qh[h][:, q0 + off:q0 + off + nn],
                            True, False), waits=(w0 if idx == 0 else []), inc=False)
                    tok_sc = None
                    pcs = pieces(qs)
                    for idx, (off, nn) in enumerate(pcs):
                        tok_sc = emit("tensor", mm(
                            sc[:, off:off + nn], ones1b[0:1, :],
                            crow_sb[cb][0:1, q0 + off:q0 + off + nn],
                            False, True), inc=(idx == len(pcs) - 1))
                    if kt >= 1:
                        emit_pv(kt - 1)
                    t_mk = None
                    if kt * 128 >= q0:
                        t_mk = emit("vector", lambda e, kt=kt, qs=qs, sc=sc: e.tensor_tensor(
                            sc[:, qs:qs + 128], sc[:, qs:qs + 128],
                            lowtri[:, :], ALU.add),
                            waits=[tok_sc], inc=True)
                    t_exp = emit("scalar", lambda e, kt=kt, qs=qs, sc=sc, h=h: e.activation(
                        expT[kt % 2][:, qs:QH], sc[:, qs:QH], AF.Exp,
                        bias=alibi[:, h * 16 + kt:h * 16 + kt + 1],
                        scale=INV_SQRT_D),
                        waits=tmax(tok_sc, t_mk, pv_toks.get(kt - 2)), inc=True)
                    exp_toks[kt] = t_exp
                    eff_toks[kt] = t_exp
                emit_pv(Kt - 1)
                tln1 = emit("scalar", lambda e: e.activation(
                    rep_sb[:, 0:512], psD1[:, :], AF.Ln, bias=0.0, scale=1.0),
                    waits=tmax(pv_toks[Kt - 1], rep_use[0]), inc=True)
                tln2 = emit("scalar", lambda e: e.activation(
                    rep_sb[:, 512:QH], psD2[:, :], AF.Ln, bias=0.0, scale=1.0),
                    waits=tmax(rep_use[1]), inc=True)
                trc1 = emit("scalar", lambda e: e.activation(
                    rep_sb[:, 0:512], rep_sb[:, 0:512], AF.Exp,
                    bias=0.0, scale=-1.0), waits=[tln1], inc=True)
                trc2 = emit("scalar", lambda e: e.activation(
                    rep_sb[:, 512:QH], rep_sb[:, 512:QH], AF.Exp,
                    bias=0.0, scale=-1.0), waits=[tln2], inc=True)
                bank["D1"] = tln1
                bank["D2"] = tln2
                t_ao1 = emit("vector", lambda e, ai=ai: e.tensor_tensor(
                    ao_sb[ai % 2][:, 0:512], psPV[:, 0:512], rep_sb[:, 0:512],
                    ALU.mult), waits=tmax(trc1, ao_store.get(ai - 2)), inc=True)
                t_ao = emit("vector", lambda e, ai=ai: e.tensor_tensor(
                    ao_sb[ai % 2][:, 512:QH], psPV[:, 512:QH], rep_sb[:, 512:QH],
                    ALU.mult), waits=[trc2], inc=True)
                rep_use = (t_ao1, t_ao)
                if qh == 0:
                    ao_dst = d_aoscr[0][h * 128:(h + 1) * 128, :]
                elif h < 3:
                    ao_dst = d_aoscr[1][h * 128:(h + 1) * 128, :]
                elif h == 3:
                    ao_dst = d_aoscr[3][0:128, :]
                else:
                    ao_dst = d_aoscr[2][(h - 4) * 128:(h - 3) * 128, :]
                if ii + 1 < len(sched):
                    nqh, nh = sched[ii + 1]
                    tdc_toks[ii + 1] = emit(
                        "sync", dma(crow_sb[(ii + 1) % 2][0:1, 0:S],
                                    d_crow[0:1, nh * S:(nh + 1) * S]),
                        waits=[crow_use[(ii + 1) % 2]], inc=True,
                        dsem=f"sc{(ii + 1) % 2}")
                t_st = emit("sync", dma(ao_dst, ao_sb[ai % 2][:, :]),
                            waits=[t_ao], inc=True, dsem=f"ga{ai % 2}")
                ao_store[ai] = t_st
                ai += 1
                crow_use[cb] = tok_sc
                bank["SCf"][Kt % 2] = exp_toks[Kt - 2]
                bank["SCf"][(Kt - 1) % 2] = exp_toks[Kt - 1]
                bank["PV"] = t_ao
                for n4 in ("A0", "A1", "B0", "B1"):
                    bank[n4] = None
                wo_cast(ii)
                if ii + 2 < NC:
                    wo_load(ii + 2)
                if ii == 7:
                    tok_cc["A"] = emit("gpsimd", lambda e: e.collective_compute(
                        "AllGather", ALU.bypass,
                        replica_groups=[[0, 1], [2, 3], [4, 5], [6, 7]],
                        ins=[d_aoscr[0].ap().opt()], outs=[d_ccout[0].ap().opt()]),
                        waits=tmax(ao_store[ai - 1], ao_store[ai - 2]),
                        inc=True, dsem="ccA")
                if ii == 11:
                    tok_cc["B2"] = emit("gpsimd", lambda e: e.collective_compute(
                        "AllGather", ALU.bypass,
                        replica_groups=[[0, 1], [2, 3], [4, 5], [6, 7]],
                        ins=[d_aoscr[2].ap().opt()], outs=[d_ccout[2].ap().opt()]),
                        waits=tmax(ao_store[ai - 1], ao_store[ai - 2]),
                        inc=True, dsem="ccB2")
                if ii == 14:
                    tok_cc["B1"] = emit("gpsimd", lambda e: e.collective_compute(
                        "AllGather", ALU.bypass,
                        replica_groups=[[0, 1], [2, 3], [4, 5], [6, 7]],
                        ins=[d_aoscr[1].ap().opt()], outs=[d_ccout[1].ap().opt()]),
                        waits=tmax(ao_store[ai - 1], ao_store[ai - 2]),
                        inc=True, dsem="ccB1")
        tok_cc["B1b"] = emit("gpsimd", lambda e: e.collective_compute(
            "AllGather", ALU.bypass,
            replica_groups=[[0, 1], [2, 3], [4, 5], [6, 7]],
            ins=[d_aoscr[3].ap().opt()], outs=[d_ccout[3].ap().opt()]),
            waits=tmax(ao_store[ai - 1], ao_store[ai - 2]),
            inc=True, dsem="ccB1b")
        P.flush_block()

        ctxm.close()

        borb = en(nc.sbuf_tensor("borb", [1, E], BF16))
        wu_b = en(nc.sbuf_tensor("wu_b", [128, E], BF16))
        bu_b = en(nc.sbuf_tensor("bu_b", [8, E], BF16))
        hpr_my = en(nc.sbuf_tensor("hpr_my", [128, QH], BF16))
        wTb_my = en(nc.sbuf_tensor("wTb_my", [8, QH], BF16))
        hprf = en(nc.sbuf_tensor("hprf", [128, S], BF16))
        wtbf = en(nc.sbuf_tensor("wtbf", [8, S], BF16))
        aotl = [en(nc.sbuf_tensor(f"aotl{i}", [128, 2048], BF16)) for i in range(2)]
        aoth = [en(nc.sbuf_tensor(f"aoth{i}", [128, 2048], BF16)) for i in range(2)]
        aot = [en(nc.sbuf_tensor(f"aot{i}", [128, 2048], BF16)) for i in range(2)]
        xrs = [en(nc.sbuf_tensor(f"xrs{i}", [128, 2048], F32)) for i in range(2)]
        outb = [en(nc.sbuf_tensor(f"outb{i}", [128, 2048], F32)) for i in range(2)]
        gsil = [en(nc.sbuf_tensor(f"gsil{i}", [128, 512], F32)) for i in range(2)]
        t_hl = emit("sync", dma(hprf[:, :], d_hscr[:, :]), inc=True, dsem="sdc")
        t_wl = emit("sync", dma(wtbf[:, :], d_wtscr[:, :]), inc=True, dsem="sdc")
        t_hsel0 = emit("vector", lambda e: e.tensor_scalar(
            hpr_my[:, :], hprf[:, 0:QH], sela[:, 0:1], None, ALU.mult),
            waits=[t_wl], inc=True)
        t_hsel = emit("vector", lambda e: e.scalar_tensor_tensor(
            hpr_my[:, :], hprf[:, QH:S], selb[:, 0:1], hpr_my[:, :],
            ALU.mult, ALU.add), waits=[t_hsel0], inc=True)
        t_wsel0 = emit("vector", lambda e: e.tensor_scalar(
            wTb_my[:, :], wtbf[0:8, 0:QH], sela[0:8, 0:1], None, ALU.mult),
            waits=[t_hsel], inc=True)
        t_wsel = emit("vector", lambda e: e.scalar_tensor_tensor(
            wTb_my[:, :], wtbf[0:8, QH:S], selb[0:8, 0:1], wTb_my[:, :],
            ALU.mult, ALU.add), waits=[t_wsel0], inc=True)
        t_bo = emit("sync", dma(stage[0][0:1, 0:E], d_bor[:, :]), inc=True, dsem="sd0")
        t_bu = emit("sync", dma(stage[0][64:72, 0:E], d_bu[:, :]), inc=True, dsem="sd0")
        t_wu = emit("sync", dma(stage[1][:, 0:E], d_wu[:, :]), inc=True, dsem="sd1")
        t_bob = emit("vector", lambda e: e.tensor_copy(borb[:, :], stage[0][0:1, 0:E]),
                     waits=[t_bu], inc=True)
        t_bub = emit("vector", lambda e: e.tensor_copy(bu_b[:, :], stage[0][64:72, 0:E]),
                     waits=[t_bu], inc=True)
        t_wub = emit("vector", lambda e: e.tensor_copy(wu_b[:, :], stage[1][:, 0:E]),
                     waits=[t_wu], inc=True)
        for name, *_ in REGS:
            bank[name] = None
        aot_use = [None, None]
        sel_use = [None, None]
        xrs_use = [None, None]
        outb_use = [None, None]
        gsil_use = [None, None]
        aot_dma = {}
        out_toks = []
        rr = 0
        t_out = None

        def aot_load(t):
            j = t % 2
            tk = None
            for hh in range(2):
                src = bass.AP(d_ccout[0], (hh * 8 * 128) * QH + t * 128,
                              [[QH, 128], [128 * QH, 8], [1, 128]])
                dst = bass.AP(aotl[j], hh * 1024, [[2048, 128], [128, 8], [1, 128]])
                tk = emit("sync", dma(dst, src),
                          waits=(tmax(sel_use[j], tok_cc["A"]) if hh == 0 else []),
                          inc=True, dsem=f"sa{j}")
            for part, rowbase, c0, nch, ccn in (
                    (2, 0, 4, 4, "B2"), (2, 512, 12, 4, "B2"),
                    (1, 0, 0, 3, "B1"), (1, 384, 8, 3, "B1"),
                    (3, 0, 3, 1, "B1b"), (3, 128, 11, 1, "B1b")):
                src = bass.AP(d_ccout[part], rowbase * QH + t * 128,
                              [[QH, 128], [128 * QH, nch], [1, 128]])
                dst = bass.AP(aoth[j], c0 * 128, [[2048, 128], [128, nch], [1, 128]])
                tk = emit("sync", dma(dst, src),
                          waits=(tmax(tok_cc[ccn]) if (t == 0 and c0 in (0, 3, 4))
                                 else []),
                          inc=True, dsem=f"sa{j}")
            aot_dma[t] = tk

        sel_tok = {}

        def emit_sel(t):
            j = t % 2
            ts0 = emit("vector", lambda e, j=j: e.tensor_scalar(
                aot[j][:, :], aotl[j][:, :], sela[:, 0:1], None, ALU.mult),
                waits=tmax(aot_dma[t], aot_use[j]), inc=True)
            ts1 = emit("vector", lambda e, j=j: e.scalar_tensor_tensor(
                aot[j][:, :], aoth[j][:, :], selb[:, 0:1], aot[j][:, :],
                ALU.mult, ALU.add), waits=[ts0], inc=True)
            sel_use[j] = ts1
            sel_tok[t] = ts1

        aot_load(0)
        emit_sel(0)
        for t in range(8):
            j = t % 2
            taot = sel_tok[t]
            txr = emit("sync", dma(xrs[j][:, :], d_xres[t * 128:(t + 1) * 128, :]),
                       waits=[xrs_use[j]], inc=True, dsem=f"sx{j}")
            if t + 1 < 8:
                aot_load(t + 1)
            f_toks = []
            for f in range(4):
                nS, ptS, oS = REGS[rr % 4]
                rr += 1
                regS = ptS[:, oS:oS + 512]
                nG, ptG, oG = REGS[rr % 4]
                rr += 1
                regG = ptG[:, oG:oG + 512]
                tok = None
                for c in range(NC):
                    w = tmax(bank[nS], wo_tok[c], taot, t_bob) if c == 0 else [wo_tok[c]]
                    tok = emit("tensor", mm(
                        regS, aot[j][:, c * 128:(c + 1) * 128],
                        wo_b[:, c * E + f * 512:c * E + f * 512 + 512],
                        c == 0, False), waits=w, inc=False)
                t_s = emit("tensor", mm(regS, ones1b[0:1, :],
                                        borb[0:1, f * 512:(f + 1) * 512], False, True),
                           inc=True)
                w = tmax(bank[nG], t_wub, t_bub, t_wsel)
                emit("tensor", mm(regG, hpr_my[:, t * 128:(t + 1) * 128],
                                  wu_b[:, f * 512:(f + 1) * 512], True, False),
                     waits=w, inc=False)
                t_g = emit("tensor", mm(regG, wTb_my[0:8, t * 128:(t + 1) * 128],
                                        bu_b[0:8, f * 512:(f + 1) * 512], False, True),
                           inc=True)
                jf = f % 2
                t_sil = emit("scalar", lambda e, jf=jf, regG=regG: e.activation(
                    gsil[jf][:, :], regG, AF.Silu, bias=0.0, scale=1.0),
                    waits=tmax(t_g, gsil_use[jf]), inc=True)
                t_mu = emit("vector", lambda e, j=j, f=f, jf=jf, regS=regS: e.tensor_tensor(
                    outb[j][:, f * 512:(f + 1) * 512], regS, gsil[jf][:, :], ALU.mult),
                    waits=tmax(t_s, t_sil, outb_use[j]), inc=True)
                t_ad = emit("vector", lambda e, j=j, f=f: e.tensor_tensor(
                    outb[j][:, f * 512:(f + 1) * 512],
                    outb[j][:, f * 512:(f + 1) * 512],
                    xrs[j][:, f * 512:(f + 1) * 512], ALU.add),
                    waits=tmax(t_mu, txr), inc=True)
                f_toks.append(t_ad)
                bank[nS] = t_mu
                bank[nG] = t_sil
                gsil_use[jf] = t_mu
                aot_use[j] = t_s
                xrs_use[j] = t_ad
                if f == 1 and t + 1 < 8:
                    emit_sel(t + 1)
            t_out = emit("gpsimd", dma(d_out[t * 128:(t + 1) * 128, :], outb[j][:, :]),
                         waits=tmax(*f_toks), inc=True, dsem=f"go{j}")
            out_toks.append(t_out)
            outb_use[j] = t_out
        P.waits("gpsimd", [out_toks[-1], out_toks[-2]])
        P.flush_block()

    return nc


_CACHE = {}


def kernel(L0_attn_x, L0_attn_Wq, L0_attn_bq, L0_attn_Wk, L0_attn_bk,
           L0_attn_Wv, L0_attn_bv, L0_attn_Wo, L0_attn_bo,
           L0_ffn_Wr, L0_ffn_br, L0_ffn_Wd, L0_ffn_bd, L0_ffn_Wu, L0_ffn_bu):
    x = np.ascontiguousarray(np.asarray(L0_attn_x, dtype=np.float32))
    Wq = np.asarray(L0_attn_Wq, np.float32)
    Wk = np.asarray(L0_attn_Wk, np.float32)
    Wv = np.asarray(L0_attn_Wv, np.float32)
    Wo = np.asarray(L0_attn_Wo, np.float32)
    bq = np.asarray(L0_attn_bq, np.float32)
    bk = np.asarray(L0_attn_bk, np.float32)
    bv = np.asarray(L0_attn_bv, np.float32)
    bo = np.asarray(L0_attn_bo, np.float32)
    Wr = np.asarray(L0_ffn_Wr, np.float32)
    br = np.asarray(L0_ffn_br, np.float32)
    Wd = np.asarray(L0_ffn_Wd, np.float32)
    bd = np.asarray(L0_ffn_bd, np.float32)
    Wu = np.asarray(L0_ffn_Wu, np.float32)
    bu = np.asarray(L0_ffn_bu, np.float32)

    crow = _build_crow()
    alibi, lt, rep, id8, id128 = _host_consts()

    def chunk_major(wt):
        n = wt.shape[1]
        return np.ascontiguousarray(
            wt.reshape(NC, 128, n).transpose(1, 0, 2).reshape(128, NC * n))

    WqT = np.ascontiguousarray(Wq.T)
    WkT = np.ascontiguousarray(Wk.T)
    WvT = np.ascontiguousarray(Wv.T)
    WoT = np.ascontiguousarray(Wo.T)
    wq_h = WqT.reshape(NC, 128, H, D).transpose(1, 2, 0, 3)
    wk_h = WkT.reshape(NC, 128, H, D).transpose(1, 2, 0, 3)
    wv_h = WvT.reshape(NC, 128, 4, 512).transpose(1, 2, 0, 3)
    wo_h = chunk_major(WoT)
    w33 = np.zeros((NC, 128, 33), np.float32)
    w33[:, :, 0:8] = Wr.reshape(NC, 128, 8)
    w33[:, :, 32] = 1.0
    wr_h = np.ascontiguousarray(w33.transpose(1, 0, 2).reshape(128, NC * 33))
    Wd_flat = np.ascontiguousarray(Wd.transpose(1, 0, 2).reshape(E, 128))
    wd_h = chunk_major(Wd_flat)
    wu_h = np.ascontiguousarray(Wu.reshape(128, E))
    bu_h = np.ascontiguousarray(bu)
    bqc = bq.reshape(H, 128).T
    bkc = bk.reshape(H, 128).T

    shared = {
        "wo": wo_h, "bor": bo.reshape(1, E).copy(), "lt": lt,
        "wr": wr_h, "br": br.reshape(8, 1).copy(),
        "wd": wd_h, "bd": bd.reshape(128, 1).copy(),
        "wu": wu_h, "bu": bu_h,
        "repm": rep, "id8": id8, "id128": id128,
    }
    hgv = []
    for hg in range(2):
        hs = slice(hg * NH, (hg + 1) * NH)
        sel = np.float32(1.0) if hg == 0 else np.float32(0.0)
        hgv.append({
            "wq": np.ascontiguousarray(wq_h[:, hs].reshape(128, NH * NC * D)),
            "wk": np.ascontiguousarray(wk_h[:, hs].reshape(128, NH * NC * D)),
            "wv": np.ascontiguousarray(
                wv_h[:, hg * 2:(hg + 1) * 2].reshape(128, 2 * NC * 512)),
            "bqc": np.ascontiguousarray(bqc[:, hs]),
            "bkc": np.ascontiguousarray(bkc[:, hs]),
            "bvr": bv[hg * NH * D:(hg + 1) * NH * D].reshape(1, NH * D).copy(),
            "alibi": np.ascontiguousarray(alibi[:, hg * NH * 16:(hg + 1) * NH * 16]),
            "crow": np.ascontiguousarray(crow[:, hg * NH * S:(hg + 1) * NH * S]),
            "sela": np.full((128, 1), 1.0 - hg, np.float32),
            "selb": np.full((128, 1), float(hg), np.float32),
        })

    key = "v2"
    if key not in _CACHE:
        _CACHE[key] = build_nc()
    nc = _CACHE[key]

    in_maps = []
    for c in range(8):
        b, hg = c // 2, c % 2
        xT = np.ascontiguousarray(x[b].T)
        m = dict(shared)
        m.update(hgv[hg])
        m["xfm"] = np.ascontiguousarray(
            xT.reshape(NC, 128, S).transpose(1, 0, 2).reshape(128, NC * S))
        m["xres"] = np.ascontiguousarray(x[b, hg * QH:(hg + 1) * QH])
        in_maps.append(m)

    res = run_bass_kernel_spmd(nc, in_maps, core_ids=list(range(8)))
    out = np.empty((4, S, E), np.float32)
    for c in range(8):
        b, hg = c // 2, c % 2
        out[b, hg * QH:(hg + 1) * QH] = res.results[c]["out"]
    return out
